# version 26
# baseline (speedup 1.0000x reference)
import contextlib
import numpy as np
import ml_dtypes
import concourse.bass as bass
import concourse.mybir as mybir
from concourse.alu_op_type import AluOpType as ALU
from concourse.bass_utils import run_bass_kernel_spmd

F32 = mybir.dt.float32
BF16 = mybir.dt.bfloat16
AF = mybir.ActivationFunctionType
AX = mybir.AxisListType

H = 16
KV_LORA = 512
Q_LORA = 1536
D = 2048
E = 16
FF = 768
SCALE = float(192.0 ** (-0.5))
RMS_EPS = 1e-6
LN_EPS = 1e-5
NEG = -1.0e30
ROPE_THETA = 500000.0

NQAC = Q_LORA // 128
NQBF = 24
NKVC = KV_LORA // 128
NKVF = 32
NDC = D // 128
NFF = FF // 128


def _deint():
    return np.concatenate([np.arange(0, 64, 2), np.arange(1, 64, 2)])


def host_prep(inputs, T):
    d = {}
    cols = np.arange(H * 192).reshape(H, 192)
    qbp = np.concatenate([cols[:, :128].reshape(-1),
                          cols[:, 128:][:, _deint()].reshape(-1)])
    cols = np.arange(H * 256).reshape(H, 256)
    kvbp = np.concatenate([cols[:, :128].reshape(-1), cols[:, 128:].reshape(-1)])
    kvap = np.concatenate([np.arange(KV_LORA), KV_LORA + _deint()])

    qa = inputs["L0_attn_q_a_w"]
    qb = np.ascontiguousarray(inputs["L0_attn_q_b_w"][:, qbp])
    kva = np.ascontiguousarray(inputs["L0_attn_kv_a_w"][:, kvap])
    kvb = np.ascontiguousarray(inputs["L0_attn_kv_b_w"][:, kvbp])
    ow = inputs["L0_attn_o_w"]

    d["qa_w"] = np.ascontiguousarray(
        qa.reshape(NDC, 128, NQAC, 128).transpose(2, 1, 0, 3)
        .reshape(NQAC, 128, NDC * 128))
    d["qb_w"] = np.ascontiguousarray(
        qb.reshape(NQAC, 128, NQBF, 128).transpose(2, 1, 0, 3)
        .reshape(NQBF, 128, NQAC * 128))
    d["kva_w"] = np.ascontiguousarray(
        kva.reshape(NDC, 128, 576).transpose(1, 0, 2).reshape(128, NDC * 576))
    d["kvb_w"] = np.ascontiguousarray(
        kvb.reshape(NKVC, 128, H * 256).transpose(1, 0, 2).reshape(128, NKVC * H * 256))
    d["o_w"] = np.ascontiguousarray(
        ow.reshape(NDC, 128, NDC, 128).transpose(2, 1, 0, 3).reshape(NDC, 128, NDC * 128))
    d["r_w"] = np.ascontiguousarray(
        inputs["L0_ffn_router_w"].reshape(NDC, 128, E).transpose(1, 0, 2)
        .reshape(128, NDC * E))
    d["gate_w"] = np.ascontiguousarray(
        inputs["L0_ffn_gate_w"].reshape(E, NDC, 128, NFF, 128)
        .transpose(0, 3, 2, 1, 4).reshape(E, NFF, 128, NDC * 128))
    d["up_w"] = np.ascontiguousarray(
        inputs["L0_ffn_up_w"].reshape(E, NDC, 128, NFF, 128)
        .transpose(0, 3, 2, 1, 4).reshape(E, NFF, 128, NDC * 128))
    d["down_w"] = np.ascontiguousarray(
        inputs["L0_ffn_down_w"].reshape(E, NFF, 128, 4, 512)
        .transpose(0, 3, 2, 1, 4).reshape(E, 4, 128, NFF * 512))
    d["qa_ln"] = np.ascontiguousarray(inputs["L0_attn_q_a_ln_w"].reshape(NQAC, 128))
    d["kva_ln"] = np.ascontiguousarray(inputs["L0_attn_kv_a_ln_w"].reshape(NKVC, 128))

    inv = 1.0 / ROPE_THETA ** (np.arange(0, 64, 2, dtype=np.float32) / 64)
    fr = np.outer(np.arange(T, dtype=np.float32), inv)
    emb = np.concatenate([fr, fr], axis=-1)
    cos = np.cos(emb).astype(np.float32).T
    sin = np.sin(emb).astype(np.float32).T
    d["c_cos"] = np.ascontiguousarray(np.concatenate([cos, cos], 0))
    d["c_sin"] = np.ascontiguousarray(np.concatenate([sin, sin], 0))
    P = np.zeros((128, 128), np.float32)
    for b0 in (0, 64):
        for i in range(32):
            P[b0 + i, b0 + i + 32] = -1.0
            P[b0 + i + 32, b0 + i] = 1.0
    d["c_rot"] = np.ascontiguousarray(P.T)
    M = np.zeros((64, 128), np.float32)
    for m in range(128):
        M[m % 64, m] = 1.0
    d["c_dup"] = M
    dm = np.zeros((128, 4 * 512), np.float32)
    for j in range(4):
        for k in range(128):
            dm[k, j * 512 + j * 128 + k:(j + 1) * 512] = 1.0
    d["c_dmask"] = dm.astype(ml_dtypes.bfloat16)
    d["c_ident_bf"] = np.eye(128, dtype=ml_dtypes.bfloat16)
    d["c_ident_f32"] = np.eye(128, dtype=np.float32)
    d["c_ones_bf"] = np.ones((128, 128), ml_dtypes.bfloat16)
    lt = np.triu(np.ones((128, 128), np.float32))
    d["c_ltri"] = lt.astype(ml_dtypes.bfloat16)
    d["c_erow"] = np.broadcast_to(np.arange(E, dtype=np.float32), (128, E)).copy()
    d["c_onesrow"] = np.ones((1, 128), np.float32)
    return d


class Sem:
    __slots__ = ("h", "name", "v", "owner", "dma")

    def __init__(self, h, name):
        self.h = h
        self.name = name
        self.v = 0
        self.owner = None
        self.dma = False


class Rec:
    def __init__(self, name, chain=None):
        self.name = name
        self.ops = []
        self.maxwait = {}
        self.chain = chain

    def wait_abs(self, sem, v):
        if v <= 0:
            return
        if sem.owner == self.name and not sem.dma:
            return
        if self.maxwait.get(id(sem), -1) >= v:
            return
        self.maxwait[id(sem)] = v
        self.ops.append(("w", sem.h, v))

    def wait(self, sem):
        self.wait_abs(sem, sem.v)

    def wait_force(self, sem, v=None):
        v = sem.v if v is None else v
        if v > 0 and self.maxwait.get(id(sem), -1) < v:
            self.maxwait[id(sem)] = v
            self.ops.append(("w", sem.h, v))

    def op(self, meth, *a, inc=None, **k):
        if self.chain is not None and "dma" not in meth and meth != "wait_ge":
            assert inc is None
            self.incop(self.chain, meth, *a, **k)
            return
        self.ops.append(("o", meth, a, k, inc))

    def incop(self, sem, meth, *a, **k):
        if "dma" in meth:
            sem.dma = True
            n = 16
        else:
            n = 1
            assert self.chain is None or sem is self.chain
        if sem.owner is None:
            sem.owner = self.name
        if self.chain is not None and "dma" not in meth and self.chain.v > 0:
            self.ops.append(("w", self.chain.h, self.chain.v))
        sem.v += n
        self.ops.append(("o", meth, a, k, (sem.h, n)))

    def replay(self, eng):
        for o in self.ops:
            if o[0] == "w":
                eng.wait_ge(o[1], o[2])
            else:
                _, meth, a, k, inc = o
                try:
                    r = getattr(eng, meth)(*a, **k)
                except Exception:
                    import sys
                    print(f"REPLAY FAIL {self.name} {meth} args={a} kw={k}",
                          file=sys.stderr)
                    raise
                if inc is not None:
                    r.then_inc(inc[0], inc[1])


class Buf:

    def __init__(self):
        self.ready = None
        self.free = []

    def p(self, R):
        for ev in self.free:
            R.wait_abs(*ev)

    def set_ready(self, sem):
        self.ready = (sem, sem.v)
        self.free = []

    def c(self, R):
        if self.ready:
            R.wait_abs(*self.ready)

    def add_free(self, sem):
        self.free.append((sem, sem.v))


class Builder:
    def __init__(self, T, dbg=False, max_phases=99):
        self.T = T
        self.dbg = dbg
        self.max_phases = max_phases
        self.phase_no = 0
        self.nc = bass.Bass("TRN2", target_bir_lowering=False)
        self.sems = {}
        self.stack = contextlib.ExitStack()
        self.NTT = T // 512
        self.NRT = T // 128

    def S(self, name):
        if name not in self.sems:
            self.sems[name] = Sem(self.nc.alloc_semaphore(name=name), name)
        return self.sems[name]

    def _u(self, n):
        self._uc = getattr(self, "_uc", 0) + 1
        return f"{n}_{self._uc}"

    def sbufu(self, name, shape, dt):
        return self.nc.sbuf_tensor(self._u(name), shape, dt)

    def psumu(self, name, shape, dt):
        return self.nc.psum_tensor(self._u(name), shape, dt)

    def new_phase(self):
        self.TE = Rec("TE")
        self.VE = Rec("VE", chain=self.S("v_p"))
        self.SE = Rec("SE", chain=self.S("se_p"))
        self.SY = Rec("SY")
        self.GP = Rec("GP")

    def emit_phase(self):
        self.phase_no += 1
        if self.phase_no > self.max_phases:
            return
        for rec in (self.SY, self.GP):
            for sem in self.sems.values():
                if sem.dma and sem.v > 0:
                    rec.wait_abs(sem, sem.v)
        with self.nc.Block() as blk:
            for rec, reg in ((self.TE, blk.tensor), (self.VE, blk.vector),
                             (self.SE, blk.scalar), (self.SY, blk.sync),
                             (self.GP, blk.gpsimd)):
                if rec.ops:
                    reg(lambda e, r=rec: r.replay(e))

    def build(self):
        nc, T = self.nc, self.T
        NTT, NRT = self.NTT, self.NRT
        S = self.S

        def din(name, shape, dt):
            return nc.dram_tensor(name, shape, dt, kind="ExternalInput").ap()

        def dscr(name, shape, dt, force_out=False):
            kind = "ExternalOutput" if (force_out or self.dbg) else "Internal"
            return nc.dram_tensor(name, shape, dt, kind=kind).ap()

        x = din("x", [T, D], F32)
        qa_w = din("qa_w", [NQAC, 128, NDC * 128], F32)
        qb_w = din("qb_w", [NQBF, 128, NQAC * 128], F32)
        kva_w = din("kva_w", [128, NDC * 576], F32)
        kvb_w = din("kvb_w", [128, NKVC * H * 256], F32)
        o_w = din("o_w", [NDC, 128, NDC * 128], F32)
        r_w = din("r_w", [128, NDC * E], F32)
        gate_w = din("gate_w", [E, NFF, 128, NDC * 128], F32)
        up_w = din("up_w", [E, NFF, 128, NDC * 128], F32)
        down_w = din("down_w", [E, 4, 128, NFF * 512], F32)
        qa_ln = din("qa_ln", [NQAC, 128], F32)
        kva_ln = din("kva_ln", [NKVC, 128], F32)
        c_cos = din("c_cos", [128, T], F32)
        c_sin = din("c_sin", [128, T], F32)
        c_rot = din("c_rot", [128, 128], F32)
        c_dup = din("c_dup", [64, 128], F32)
        c_dmask = din("c_dmask", [128, 4 * 512], BF16)
        c_ident_bf = din("c_ident_bf", [128, 128], BF16)
        c_ident_f32 = din("c_ident_f32", [128, 128], F32)
        c_ones_bf = din("c_ones_bf", [128, 128], BF16)
        c_ltri = din("c_ltri", [128, 128], BF16)
        c_erow = din("c_erow", [128, E], F32)
        c_onesrow = din("c_onesrow", [1, 128], F32)

        d_kn = dscr("d_kn", [H * 128, T], BF16)
        d_qT = dscr("d_qT", [NTT, 128, NQBF * 512], BF16)
        d_y1bf = nc.dram_tensor("d_y1bf", [T, D], BF16, kind="Internal").ap()
        d_gidx = nc.dram_tensor("d_gidx", [128, 48], mybir.dt.uint32,
                                kind="Internal").ap()
        d_expout = nc.dram_tensor("d_expout", [E * 384 + 128, D], BF16,
                                  kind="Internal").ap()
        out = dscr("out", [T, D], F32, force_out=True)

        sb = lambda n, s, dt: self.stack.enter_context(nc.sbuf_tensor(n, s, dt))

        ident_bf = sb("ident_bf", [128, 128], BF16)
        ident_f32 = sb("ident_f32", [128, 128], F32)
        ones_bf = sb("ones_bf", [128, 128], BF16)
        rot_t = sb("rot_t", [128, 128], F32)
        dup_t = sb("dup_t", [64, 128], BF16)
        qa_ln_t = sb("qa_ln_t", [128, NQAC], F32)
        kva_ln_t = sb("kva_ln_t", [128, NKVC], F32)
        P1 = sb("P1", [128, max(16 * T, 28672)], BF16)
        P2 = sb("P2", [128, 24576], BF16)
        stn = ["lnsum", "lnsumsq", "lnmu", "lnvar", "lntmp", "lnrstd",
               "rtmax", "rtneg", "rtsum", "rtm1", "rtm2", "rtden"]
        st = {n: sb("st_" + n, [128, 1], F32) for n in stn}
        eps_rms = sb("eps_rms", [128, 1], F32)
        eps_ln = sb("eps_ln", [128, 1], F32)
        tsrc = sb("tsrc", [128, 2 * (T // 128)], mybir.dt.uint32)
        tw = sb("tw", [128, 2 * (T // 128)], F32)
        mid_stack = contextlib.ExitStack()
        krope = mid_stack.enter_context(nc.sbuf_tensor("krope", [128, T], BF16))
        mneg_sb = mid_stack.enter_context(nc.sbuf_tensor("mneg_sb", [128, H], F32))
        inner_stack = contextlib.ExitStack()
        cosT = inner_stack.enter_context(nc.sbuf_tensor("cosT", [128, T], F32))
        sinT = inner_stack.enter_context(nc.sbuf_tensor("sinT", [128, T], F32))
        kvn = inner_stack.enter_context(nc.sbuf_tensor("kvn", [128, NKVC * T], BF16))
        kr2 = inner_stack.enter_context(nc.sbuf_tensor("kr2", [1, T], BF16))
        kmax2 = inner_stack.enter_context(nc.sbuf_tensor("kmax2", [1, H], F32))

        t_p, v_p, se_p = S("t_p"), S("v_p"), S("se_p")
        s_y1w = S("s_y1w")
        s_y1s = S("s_y1s")
        s_out = S("s_out")


        ph0 = contextlib.ExitStack()
        xrow = ph0.enter_context(self.sbufu("xrow", [128, 2 * D], F32))
        xrow_bf = ph0.enter_context(self.sbufu("xrow_bf", [128, D], BF16))
        g_tmp = ph0.enter_context(self.sbufu("g_tmp", [NQAC, 128], F32))
        g_tmp2 = ph0.enter_context(self.sbufu("g_tmp2", [NKVC, 128], F32))
        b0_tr0 = ph0.enter_context(self.psumu("b0_tr0", [128, 512], BF16))
        b0_tr1 = ph0.enter_context(self.psumu("b0_tr1", [128, 512], BF16))
        if True:

            self.new_phase()
            TE, VE, SE, SY, GP = self.TE, self.VE, self.SE, self.SY, self.GP
            ld = S("b0_ld")
            for dst, src in [(ident_bf, c_ident_bf), (ident_f32, c_ident_f32),
                             (ones_bf, c_ones_bf),
                             (rot_t, c_rot),

                             (cosT, c_cos), (sinT, c_sin),
                             (g_tmp, qa_ln), (g_tmp2, kva_ln)]:
                SY.incop(ld, "dma_start", dst[:], src[:])
            ldg = S("b0_ldg")
            GP.incop(ldg, "dma_start", dup_t[:], c_dup[:])

            VE.op("memset", eps_rms[:], RMS_EPS)
            VE.op("memset", eps_ln[:], LN_EPS)
            TE.wait(ld)
            TE.wait(ldg)
            b_xrow = [Buf(), Buf()]
            b_xbf = [Buf(), Buf()]
            b_tr = [Buf(), Buf()]
            trbank = [b0_tr0, b0_tr1]
            sx = [S("b0x0"), S("b0x1")]

            def emit_b0row(r):
                p = r % 2
                b_xrow[p].p(SY)
                SY.incop(sx[p], "dma_start", xrow[:, p * D:(p + 1) * D],
                         x[r * 128:(r + 1) * 128, :])
                b_xrow[p].set_ready(sx[p])
                b_xrow[p].c(SE)
                b_xbf[0].p(SE)
                SE.incop(se_p, "activation", xrow_bf[:],
                         xrow[:, p * D:(p + 1) * D], AF.Copy)
                b_xrow[p].add_free(se_p)
                b_xbf[0].set_ready(se_p)
                b_xbf[0].c(TE)
                for g in range(4):
                    gp = (r * 4 + g) % 2
                    b_tr[gp].p(TE)
                    for j in range(4):
                        c = g * 4 + j
                        TE.incop(t_p, "transpose",
                                 trbank[gp][:, j * 128:(j + 1) * 128],
                                 xrow_bf[:, c * 128:(c + 1) * 128],
                                 ident_bf[:])
                    b_tr[gp].set_ready(t_p)
                    if g == 3:
                        b_xbf[0].add_free(t_p)
                    b_tr[gp].c(VE)
                    for j in range(4):
                        c = g * 4 + j
                        VE.incop(v_p, "tensor_copy",
                                 P1[:, c * T + r * 128: c * T + (r + 1) * 128],
                                 trbank[gp][:, j * 128:(j + 1) * 128])
                    b_tr[gp].add_free(v_p)

        ph = contextlib.ExitStack()
        kva_bf = ph.enter_context(self.sbufu("kva_bf", [128, NDC * 576], BF16))
        kvc_bf = ph.enter_context(self.sbufu("kvc_bf", [128, 4 * 512], BF16))
        sqb = ph.enter_context(self.sbufu("sqb", [128, 2 * 512], BF16))
        rstd_t = ph.enter_context(self.sbufu("rstd_t", [128, 512], F32))
        rp32 = ph.enter_context(self.sbufu("rp32", [128, 2 * 512], F32))
        rpb = ph.enter_context(self.sbufu("rpb", [64, 512], BF16))
        sq6 = ph.enter_context(self.sbufu("sq6", [64, 512], BF16))
        p_mm0 = ph.enter_context(self.psumu("p_mm0", [128, 512], F32))
        p_mm1 = ph.enter_context(self.psumu("p_mm1", [128, 512], F32))
        p_st = ph.enter_context(self.psumu("p_st", [128, 512], F32))
        p_ro = ph.enter_context(self.psumu("p_ro", [128, 512], F32))
        p_du = ph.enter_context(self.psumu("p_du", [128, 512], F32))
        p_kr = ph.enter_context(self.psumu("p_kr", [1, 512], F32))
        if True:
            wk = S("w_kva")
            GP.incop(wk, "dma_start", kva_bf[:], kva_w[:])
            TE.incop(t_p, "transpose", p_st[:, 0:NQAC], g_tmp[:],
                     ident_f32[0:NQAC, 0:NQAC])
            VE.wait(t_p)
            VE.incop(v_p, "tensor_copy", qa_ln_t[:], p_st[:, 0:NQAC])
            TE.wait(v_p)
            TE.incop(t_p, "transpose", p_st[:, 0:NKVC], g_tmp2[:],
                     ident_f32[0:NKVC, 0:NKVC])
            VE.wait(t_p)
            VE.incop(v_p, "tensor_copy", kva_ln_t[:], p_st[:, 0:NKVC])
            TE.wait(v_p)
            TE.wait(wk)
            pmm = [Buf(), Buf()]
            pbank = [p_mm0, p_mm1]
            b_kvc = Buf()
            b_sq = [Buf(), Buf()]
            b_pst = Buf()
            b_rstd = Buf()
            b_rp = Buf()
            b_pro = Buf()
            b_pdu = Buf()
            b_rpb = Buf()
            b_sq6 = Buf()
            b_pkr = Buf()
            mi = 0
            for tt in range(NTT):
                t0 = tt * 512
                for r_ in range(4 * tt, 4 * tt + 4):
                    emit_b0row(r_)
                ev_x4 = (v_p, v_p.v)
                TE.wait_abs(*ev_x4)
                b_kvc.p(SE)
                for f in range(NKVC):
                    par = mi % 2
                    pmm[par].p(TE)
                    for c in range(NDC):
                        TE.incop(t_p, "matmul", pbank[par][:],
                                 kva_bf[:, c * 576 + f * 128: c * 576 + (f + 1) * 128],
                                 P1[:, c * T + t0: c * T + t0 + 512],
                                 start=(c == 0), stop=(c == NDC - 1))
                    pmm[par].set_ready(t_p)
                    pmm[par].c(SE)
                    SE.incop(se_p, "activation", kvc_bf[:, f * 512:(f + 1) * 512],
                             pbank[par][:], AF.Copy)
                    pmm[par].add_free(se_p)
                    mi += 1
                b_kvc.set_ready(se_p)
                b_kvc.c(VE)
                for f in range(NKVC):
                    sp = f % 2
                    b_sq[sp].p(VE)
                    VE.incop(v_p, "tensor_tensor", sqb[:, sp * 512:(sp + 1) * 512],
                             kvc_bf[:, f * 512:(f + 1) * 512],
                             kvc_bf[:, f * 512:(f + 1) * 512], ALU.mult)
                    b_sq[sp].set_ready(v_p)
                    b_sq[sp].c(TE)
                    if f == 0:
                        b_pst.p(TE)
                    TE.incop(t_p, "matmul", p_st[:], ones_bf[:],
                             sqb[:, sp * 512:(sp + 1) * 512],
                             start=(f == 0), stop=(f == NKVC - 1))
                    b_sq[sp].add_free(t_p)
                b_pst.set_ready(t_p)
                b_pst.c(SE)
                b_rstd.p(SE)
                SE.op("activation", rstd_t[:], p_st[:], AF.Ln,
                      scale=1.0 / KV_LORA, bias=eps_rms[:])
                SE.incop(se_p, "activation", rstd_t[:], rstd_t[:], AF.Exp, scale=-0.5)
                b_pst.add_free(se_p)
                b_rstd.set_ready(se_p)
                b_rstd.c(VE)
                for f in range(NKVC):
                    VE.incop(v_p, "scalar_tensor_tensor",
                             kvn[:, f * T + t0: f * T + t0 + 512],
                             kvc_bf[:, f * 512:(f + 1) * 512],
                             kva_ln_t[:, f:f + 1], rstd_t[:],
                             ALU.mult, ALU.mult)
                b_rstd.add_free(v_p)
                b_kvc.add_free(v_p)

                b_pro.p(TE)
                for c in range(NDC):
                    TE.incop(t_p, "matmul", p_ro[:64, :],
                             kva_bf[:, c * 576 + 512: c * 576 + 576],
                             P1[:, c * T + t0: c * T + t0 + 512],
                             start=(c == 0), stop=(c == NDC - 1))
                b_pro.set_ready(t_p)
                b_pro.c(SE)
                b_rp.p(SE)
                SE.incop(se_p, "activation", rp32[:64, 0:512], p_ro[:64, :], AF.Copy)
                b_pro.add_free(se_p)
                b_rp.set_ready(se_p)
                b_rp.c(TE)
                b_pro.p(TE)
                TE.incop(t_p, "matmul", p_ro[:64, :], rot_t[0:64, 0:64],
                         rp32[:64, 0:512], start=True, stop=True)
                b_pro.set_ready(t_p)
                b_pro.c(VE)
                b_rpb.p(VE)
                VE.op("tensor_tensor", rp32[:64, 0:512], rp32[:64, 0:512],
                      cosT[0:64, t0:t0 + 512], ALU.mult)
                VE.op("tensor_tensor", rp32[:64, 512:1024], p_ro[:64, :],
                      sinT[0:64, t0:t0 + 512], ALU.mult)
                VE.incop(v_p, "tensor_tensor", rpb[:, :],
                         rp32[:64, 0:512], rp32[:64, 512:1024], ALU.add)
                b_pro.add_free(v_p)
                b_rp.add_free(v_p)
                b_rpb.set_ready(v_p)
                b_rpb.c(TE)
                b_pdu.p(TE)
                TE.incop(t_p, "matmul", p_du[:], dup_t[:], rpb[:, :],
                         start=True, stop=True)
                b_rpb.add_free(t_p)
                b_pdu.set_ready(t_p)
                b_pdu.c(SE)
                SE.incop(se_p, "activation", krope[:, t0:t0 + 512], p_du[:],
                         AF.Copy)
                b_pdu.add_free(se_p)
                b_sq6.p(VE)
                VE.incop(v_p, "tensor_tensor", sq6[:, :],
                         rpb[:, :], rpb[:, :], ALU.mult)
                b_rpb.add_free(v_p)
                b_sq6.set_ready(v_p)
                b_sq6.c(TE)
                b_pkr.p(TE)
                TE.incop(t_p, "matmul", p_kr[0:1, :], ones_bf[0:64, 0:1],
                         sq6[:, :], start=True, stop=True, skip_group_check=True)
                b_sq6.add_free(t_p)
                b_pkr.set_ready(t_p)
                b_pkr.c(VE)
                VE.incop(v_p, "tensor_copy", kr2[0:1, t0:t0 + 512], p_kr[0:1, :])
                b_pkr.add_free(v_p)
            SY.wait(ld)
            GP.wait(ldg)
            self.emit_phase()
            ph.close()
            ph0.close()

        ph = contextlib.ExitStack()
        qaw = ph.enter_context(self.sbufu("qaw", [128, 3 * NDC * 128], BF16))
        sqb2 = ph.enter_context(self.sbufu("sqb2", [128, 2 * 512], BF16))
        rstd2 = ph.enter_context(self.sbufu("rstd2", [128, 512], F32))
        p_mm0 = ph.enter_context(self.psumu("p_mm0", [128, 512], F32))
        p_mm1 = ph.enter_context(self.psumu("p_mm1", [128, 512], F32))
        p_st2 = ph.enter_context(self.psumu("p_st2", [128, 512], F32))
        if True:

            self.new_phase()
            TE, VE, SE, SY, GP = self.TE, self.VE, self.SE, self.SY, self.GP
            pmm = [Buf(), Buf()]
            pbank = [p_mm0, p_mm1]
            b_w = [Buf() for _ in range(3)]
            sw = [S(f"b2aw{i}") for i in range(3)]
            mi = 0
            for f in range(NQAC):
                wp = f % 3
                b_w[wp].p(GP)
                GP.incop(sw[wp], "dma_start",
                         qaw[:, wp * NDC * 128:(wp + 1) * NDC * 128], qa_w[f])
                b_w[wp].set_ready(sw[wp])
                b_w[wp].c(TE)
                for tt in range(NTT):
                    t0 = tt * 512
                    par = mi % 2
                    pmm[par].p(TE)
                    for c in range(NDC):
                        TE.incop(t_p, "matmul", pbank[par][:],
                                 qaw[:, wp * NDC * 128 + c * 128:
                                     wp * NDC * 128 + (c + 1) * 128],
                                 P1[:, c * T + t0: c * T + t0 + 512],
                                 start=(c == 0), stop=(c == NDC - 1))
                    pmm[par].set_ready(t_p)
                    if tt == NTT - 1:
                        b_w[wp].add_free(t_p)
                    pmm[par].c(SE)
                    SE.incop(se_p, "activation",
                             P2[:, f * T + t0: f * T + t0 + 512],
                             pbank[par][:], AF.Copy)
                    pmm[par].add_free(se_p)
                    mi += 1
            b_sq = [Buf(), Buf()]
            b_pst = Buf()
            b_rstd = Buf()
            VE.wait(se_p)
            for tt in range(NTT):
                t0 = tt * 512
                for f in range(NQAC):
                    sp = f % 2
                    b_sq[sp].p(VE)
                    VE.incop(v_p, "tensor_tensor", sqb2[:, sp * 512:(sp + 1) * 512],
                             P2[:, f * T + t0: f * T + t0 + 512],
                             P2[:, f * T + t0: f * T + t0 + 512], ALU.mult)
                    b_sq[sp].set_ready(v_p)
                    b_sq[sp].c(TE)
                    if f == 0:
                        b_pst.p(TE)
                    TE.incop(t_p, "matmul", p_st2[:], ones_bf[:],
                             sqb2[:, sp * 512:(sp + 1) * 512],
                             start=(f == 0), stop=(f == NQAC - 1))
                    b_sq[sp].add_free(t_p)
                b_pst.set_ready(t_p)
                b_pst.c(SE)
                b_rstd.p(SE)
                SE.op("activation", rstd2[:], p_st2[:], AF.Ln,
                      scale=1.0 / Q_LORA, bias=eps_rms[:])
                SE.incop(se_p, "activation", rstd2[:], rstd2[:], AF.Exp, scale=-0.5)
                b_pst.add_free(se_p)
                b_rstd.set_ready(se_p)
                b_rstd.c(VE)
                for f in range(NQAC):
                    VE.op("scalar_tensor_tensor",
                          P2[:, f * T + t0: f * T + t0 + 512],
                          P2[:, f * T + t0: f * T + t0 + 512],
                          qa_ln_t[:, f:f + 1], rstd2[:], ALU.mult, ALU.mult)
                b_rstd.add_free(v_p)
            if self.dbg:
                d_xan = nc.dram_tensor("dbg_xan", [128, NQAC * T], BF16,
                                       kind="ExternalOutput").ap()
                sdbg = S("sdbg")
                SY.wait(v_p)
                SY.incop(sdbg, "dma_start", d_xan[:], P2[:, 0:NQAC * T])
                SY.wait(sdbg)
            self.emit_phase()
            ph.close()

        skn = [S("b1bk0"), S("b1bk1")]
        ph = contextlib.ExitStack()
        kvb_bf = ph.enter_context(self.sbufu("kvb_bf", [128, NKVC * H * 256], BF16))
        knst = ph.enter_context(self.sbufu("knst", [128, 2 * 512], BF16))
        vst = ph.enter_context(self.sbufu("vst", [128, 2 * 512], BF16))
        sqk = ph.enter_context(self.sbufu("sqk", [128, 2 * 512], BF16))
        tkrow = ph.enter_context(self.sbufu("tkrow", [1, 512], BF16))
        p_mm0 = ph.enter_context(self.psumu("p_mm0", [128, 512], F32))
        p_mm1 = ph.enter_context(self.psumu("p_mm1", [128, 512], F32))
        p_tr0 = ph.enter_context(self.psumu("p_tr0", [128, 512], BF16))
        p_tr1 = ph.enter_context(self.psumu("p_tr1", [128, 512], BF16))
        p_kn0 = ph.enter_context(self.psumu("p_kn0", [1, 512], F32))
        p_kn1 = ph.enter_context(self.psumu("p_kn1", [1, 512], F32))
        if True:

            self.new_phase()
            TE, VE, SE, SY, GP = self.TE, self.VE, self.SE, self.SY, self.GP
            wkb = S("w_kvb")
            for c in range(NKVC):
                GP.incop(wkb, "dma_start",
                         kvb_bf[:, c * H * 256:(c + 1) * H * 256],
                         kvb_w[:, c * H * 256:(c + 1) * H * 256])
            TE.wait(wkb)
            pmm = [Buf(), Buf()]
            pbank = [p_mm0, p_mm1]
            ptr = [Buf(), Buf()]
            ptbank = [p_tr0, p_tr1]
            pknb = [p_kn0, p_kn1]
            b_knst = [Buf(), Buf()]
            b_vst = [Buf(), Buf()]
            b_sqk = [Buf(), Buf()]
            b_pkn = [Buf(), Buf()]
            pend_k = []
            mi = 0
            for tt in range(NTT):
                t0 = tt * 512
                for fc in range(NKVF):
                    par = mi % 2
                    pmm[par].p(TE)
                    for c in range(NKVC):
                        TE.incop(t_p, "matmul", pbank[par][:],
                                 kvb_bf[:, c * H * 256 + fc * 128:
                                        c * H * 256 + (fc + 1) * 128],
                                 kvn[:, c * T + t0: c * T + t0 + 512],
                                 start=(c == 0), stop=(c == NKVC - 1))
                    pmm[par].set_ready(t_p)
                    for fn_ in pend_k:
                        fn_()
                    pend_k.clear()
                    pmm[par].c(SE)
                    sp = mi % 2
                    if fc < H:
                        b_knst[sp].p(SE)
                        SE.incop(se_p, "activation",
                                 knst[:, sp * 512:(sp + 1) * 512], pbank[par][:],
                                 AF.Copy)
                        pmm[par].add_free(se_p)
                        b_knst[sp].set_ready(se_p)
                        b_knst[sp].c(SY)
                        SY.incop(skn[sp], "dma_start",
                                 d_kn[fc * 128:(fc + 1) * 128, t0:t0 + 512],
                                 knst[:, sp * 512:(sp + 1) * 512])
                        b_knst[sp].add_free(skn[sp])
                        b_sqk[sp].p(SE)
                        SE.incop(se_p, "activation",
                                 sqk[:, sp * 512:(sp + 1) * 512], pbank[par][:],
                                 AF.Square)
                        pmm[par].add_free(se_p)
                        b_sqk[sp].set_ready(se_p)

                        def kmax_fn(sp=sp, tt=tt, fc=fc):
                            b_sqk[sp].c(TE)
                            b_pkn[sp].p(TE)
                            TE.incop(t_p, "matmul", pknb[sp][0:1, :],
                                     ones_bf[:, 0:1],
                                     sqk[:, sp * 512:(sp + 1) * 512],
                                     start=True, stop=True,
                                     skip_group_check=True)
                            b_sqk[sp].add_free(t_p)
                            b_pkn[sp].set_ready(t_p)
                            b_pkn[sp].c(VE)
                            VE.incop(v_p, "tensor_scalar", tkrow[0:1, :],
                                     pknb[sp][0:1, :], 0.0,
                                     None if tt == 0 else kmax2[0:1, fc:fc + 1],
                                     ALU.add, op1=ALU.max,
                                     accum_out=kmax2[0:1, fc:fc + 1])
                            b_pkn[sp].add_free(v_p)
                        pend_k.append(kmax_fn)
                    else:
                        h = fc - H
                        b_vst[sp].p(SE)
                        SE.incop(se_p, "activation",
                                 vst[:, sp * 512:(sp + 1) * 512], pbank[par][:],
                                 AF.Copy)
                        pmm[par].add_free(se_p)
                        b_vst[sp].set_ready(se_p)
                        b_vst[sp].c(TE)
                        ptr[sp].p(TE)
                        for j in range(4):
                            TE.incop(t_p, "transpose",
                                     ptbank[sp][:, j * 128:(j + 1) * 128],
                                     vst[:, sp * 512 + j * 128: sp * 512 + (j + 1) * 128],
                                     ident_bf[:])
                        ptr[sp].set_ready(t_p)
                        b_vst[sp].add_free(t_p)
                        ptr[sp].c(VE)
                        for j in range(4):
                            kt = tt * 4 + j
                            VE.incop(v_p, "tensor_copy",
                                     P1[:, (kt * H + h) * 128:(kt * H + h + 1) * 128],
                                     ptbank[sp][:, j * 128:(j + 1) * 128])
                        ptr[sp].add_free(v_p)
                    mi += 1
            for fn_ in pend_k:
                fn_()
            pend_k.clear()
            self.emit_phase()
            ph.close()

        sqt = [S("b2bq0"), S("b2bq1")]
        ph = contextlib.ExitStack()
        qbw = ph.enter_context(self.sbufu("qbw", [128, 2 * NQAC * 128], BF16))
        qst = ph.enter_context(self.sbufu("qst", [128, 2 * 512], BF16))
        sqn = ph.enter_context(self.sbufu("sqn", [128, 2 * 512], BF16))
        qr32 = ph.enter_context(self.sbufu("qr32", [128, 2 * 512], F32))
        qtmp = ph.enter_context(self.sbufu("qtmp", [128, 512], F32))
        tkq = ph.enter_context(self.sbufu("tkq", [1, 512], BF16))
        qmx = ph.enter_context(self.sbufu("qmx", [1, 2 * H], F32))
        krmax = ph.enter_context(self.sbufu("krmax", [1, 1], F32))
        mrow = ph.enter_context(self.sbufu("mrow", [1, 2 * H], F32))
        mrow_bf = ph.enter_context(self.sbufu("mrow_bf", [1, H], BF16))
        p_mm0 = ph.enter_context(self.psumu("p_mm0", [128, 512], F32))
        p_mm1 = ph.enter_context(self.psumu("p_mm1", [128, 512], F32))
        p_ro4 = ph.enter_context(self.psumu("p_ro4", [128, 512], F32))
        p_m0 = ph.enter_context(self.psumu("p_m0", [1, 512], F32))
        p_m1 = ph.enter_context(self.psumu("p_m1", [1, 512], F32))
        p_mb = ph.enter_context(self.psumu("p_mb", [128, H], F32))
        if True:

            self.new_phase()
            TE, VE, SE, SY, GP = self.TE, self.VE, self.SE, self.SY, self.GP
            VE.incop(v_p, "tensor_reduce", krmax[0:1, :], kr2[0:1, :],
                     AX.X, ALU.max)
            pmm = [Buf(), Buf()]
            pbank = [p_mm0, p_mm1]
            pmb_ = [p_m0, p_m1]
            b_w = [Buf(), Buf()]
            sw = [S("b2bw0"), S("b2bw1")]
            b_qst = [Buf(), Buf()]
            b_sqn = [Buf(), Buf()]
            b_pm = [Buf(), Buf()]
            b_qr = [Buf(), Buf()]
            b_qtmp = Buf()
            b_pro = Buf()
            mi = 0
            pmi = 0

            pend_q = []

            def qmax_col(sqn_ap, ones_ap, col, seed, bfree):
                nonlocal pmi
                pm = pmi % 2
                pmi += 1
                b_pm[pm].p(TE)
                TE.incop(t_p, "matmul", pmb_[pm][0:1, :], ones_ap, sqn_ap,
                         start=True, stop=True, skip_group_check=True)
                if bfree is not None:
                    bfree.add_free(t_p)
                b_pm[pm].set_ready(t_p)
                b_pm[pm].c(VE)
                VE.incop(v_p, "tensor_scalar", tkq[0:1, :], pmb_[pm][0:1, :],
                         0.0, None if seed else qmx[0:1, col:col + 1],
                         ALU.add, op1=ALU.max, accum_out=qmx[0:1, col:col + 1])
                b_pm[pm].add_free(v_p)

            for fc in range(NQBF):
                wp = fc % 2
                b_w[wp].p(GP)
                GP.incop(sw[wp], "dma_start",
                         qbw[:, wp * NQAC * 128:(wp + 1) * NQAC * 128],
                         qb_w[fc])
                b_w[wp].set_ready(sw[wp])
                b_w[wp].c(TE)
                for tt in range(NTT):
                    t0 = tt * 512
                    par = mi % 2
                    pmm[par].p(TE)
                    for c in range(NQAC):
                        TE.incop(t_p, "matmul", pbank[par][:],
                                 qbw[:, wp * NQAC * 128 + c * 128:
                                     wp * NQAC * 128 + (c + 1) * 128],
                                 P2[:, c * T + t0: c * T + t0 + 512],
                                 start=(c == 0), stop=(c == NQAC - 1))
                    pmm[par].set_ready(t_p)
                    for fn_ in pend_q:
                        fn_()
                    pend_q.clear()
                    if tt == NTT - 1:
                        b_w[wp].add_free(t_p)
                    sp = mi % 2
                    pmm[par].c(SE)
                    if fc < H:
                        b_qst[sp].p(SE)
                        SE.incop(se_p, "activation",
                                 qst[:, sp * 512:(sp + 1) * 512], pbank[par][:],
                                 AF.Copy, scale=SCALE)
                        b_qst[sp].set_ready(se_p)
                        b_sqn[sp].p(SE)
                        SE.incop(se_p, "activation",
                                 sqn[:, sp * 512:(sp + 1) * 512], pbank[par][:],
                                 AF.Square, scale=SCALE)
                        pmm[par].add_free(se_p)
                        b_sqn[sp].set_ready(se_p)
                        pend_q.append(
                            lambda a=sqn[:, sp * 512:(sp + 1) * 512],
                            c_=fc, s_=(tt == 0), b=b_sqn[sp]:
                            (b.c(TE), qmax_col(a, ones_bf[:, 0:1], c_, s_, b)))
                    else:
                        b_qr[sp].p(SE)
                        SE.incop(se_p, "activation",
                                 qr32[:, sp * 512:(sp + 1) * 512], pbank[par][:],
                                 AF.Copy, scale=SCALE)
                        pmm[par].add_free(se_p)
                        b_qr[sp].set_ready(se_p)
                        b_qr[sp].c(TE)
                        b_pro.p(TE)
                        TE.incop(t_p, "matmul", p_ro4[:], rot_t[:],
                                 qr32[:, sp * 512:(sp + 1) * 512],
                                 start=True, stop=True)
                        b_pro.set_ready(t_p)
                        b_pro.c(VE)
                        b_qtmp.p(VE)
                        VE.op("tensor_tensor", qtmp[:], p_ro4[:],
                              sinT[:, t0:t0 + 512], ALU.mult)
                        VE.op("tensor_tensor", qr32[:, sp * 512:(sp + 1) * 512],
                              qr32[:, sp * 512:(sp + 1) * 512],
                              cosT[:, t0:t0 + 512], ALU.mult)
                        b_qst[sp].p(VE)
                        VE.incop(v_p, "tensor_tensor",
                                 qst[:, sp * 512:(sp + 1) * 512],
                                 qr32[:, sp * 512:(sp + 1) * 512], qtmp[:],
                                 ALU.add)
                        b_pro.add_free(v_p)
                        b_qr[sp].add_free(v_p)
                        b_qtmp.add_free(v_p)
                        b_qst[sp].set_ready(v_p)
                        b_sqn[sp].p(SE)
                        SE.wait(v_p)
                        SE.incop(se_p, "activation",
                                 sqn[:, sp * 512:(sp + 1) * 512],
                                 qst[:, sp * 512:(sp + 1) * 512], AF.Square)
                        b_qst[sp].add_free(se_p)
                        b_sqn[sp].set_ready(se_p)
                        hA = 2 * (fc - 16)

                        def rope_q(sp=sp, hA=hA, seed=(tt == 0)):
                            b_sqn[sp].c(TE)
                            qmax_col(sqn[0:64, sp * 512:(sp + 1) * 512],
                                     ones_bf[0:64, 0:1], H + hA, seed, None)
                            qmax_col(sqn[64:128, sp * 512:(sp + 1) * 512],
                                     ones_bf[64:128, 0:1], H + hA + 1, seed,
                                     b_sqn[sp])
                        pend_q.append(rope_q)
                    b_qst[sp].c(SY)
                    SY.incop(sqt[sp], "dma_start",
                             d_qT[tt, :, fc * 512:(fc + 1) * 512],
                             qst[:, sp * 512:(sp + 1) * 512])
                    b_qst[sp].add_free(sqt[sp])
                    mi += 1
            for fn_ in pend_q:
                fn_()
            pend_q.clear()
            VE.op("tensor_scalar", mrow[0:1, 0:H], kmax2[0:1, :],
                  krmax[0:1, 0:1], None, ALU.add)
            VE.op("tensor_tensor", mrow[0:1, H:2 * H], qmx[0:1, 0:H],
                  qmx[0:1, H:2 * H], ALU.add)
            VE.incop(v_p, "tensor_tensor", mrow[0:1, 0:H], mrow[0:1, 0:H],
                     mrow[0:1, H:2 * H], ALU.mult)
            SE.wait(v_p)
            SE.incop(se_p, "activation", mrow[0:1, H:2 * H], mrow[0:1, 0:H],
                     AF.Sqrt)
            VE.wait(se_p)
            VE.incop(v_p, "tensor_scalar", mrow_bf[0:1, :], mrow[0:1, H:2 * H],
                     -1.0, None, ALU.mult)
            TE.wait(v_p)
            TE.incop(t_p, "matmul", p_mb[:, 0:H], ones_bf[0:1, 0:128],
                     mrow_bf[0:1, :], start=True, stop=True,
                     skip_group_check=True)
            SE.wait(t_p)
            SE.incop(se_p, "activation", mneg_sb[:, :], p_mb[:, 0:H], AF.Copy)
            self.emit_phase()
            ph.close()
        inner_stack.close()

        ph = contextlib.ExitStack()
        kn_sb = ph.enter_context(self.sbufu("kn_sb", [128, 2 * 2048], BF16))
        dmask_t = ph.enter_context(self.sbufu("dmask_t", [128, 4 * 512], BF16))
        ltri_t = ph.enter_context(self.sbufu("ltri_t", [128, 128], BF16))
        erow_t = ph.enter_context(self.sbufu("erow_t", [128, E], F32))
        onesrow_t = ph.enter_context(self.sbufu("onesrow_t", [1, 128], F32))
        rwt = ph.enter_context(self.sbufu("rwt", [128, NDC * E], BF16))
        prb = ph.enter_context(self.sbufu("prb", [128, 4 * E], F32))
        indall = ph.enter_context(self.sbufu("indall", [128, 4 * E], BF16))
        eqall = ph.enter_context(self.sbufu("eqall", [128, 8 * E], BF16))
        eall = ph.enter_context(self.sbufu("eall", [128, 8], F32))
        slot4 = ph.enter_context(self.sbufu("slot4", [128, 4 * E], F32))
        bases4 = ph.enter_context(self.sbufu("bases4", [1, 4 * E], F32))
        tmp16 = ph.enter_context(self.sbufu("tmp16", [128, E], F32))
        base = ph.enter_context(self.sbufu("base", [128, E], F32))
        au8 = ph.enter_context(self.sbufu("au8", [128, 8], mybir.dt.uint32))
        adrf = ph.enter_context(self.sbufu("adrf", [128, 2], F32))
        dcol = ph.enter_context(self.sbufu("dcol", [128, 4], F32))
        wrow = ph.enter_context(self.sbufu("wrow", [128, 4], F32))
        gsent = ph.enter_context(self.sbufu("gsent", [128, E * 3],
                                            mybir.dt.uint32))
        tok32 = ph.enter_context(self.sbufu("tok32", [128, 1],
                                            mybir.dt.uint32))
        eT = ph.enter_context(self.sbufu("eT", [128, 3 * 512], BF16))
        rczf = ph.enter_context(self.sbufu("rczf", [1, 512], F32))
        rcz = ph.enter_context(self.sbufu("rcz", [1, 2 * 512], BF16))
        zbc = ph.enter_context(self.sbufu("zbc", [128, 2 * 512], BF16))
        owt = ph.enter_context(self.sbufu("owt", [128, 3 * 2048], BF16))
        oevac = ph.enter_context(self.sbufu("oevac", [128, 2 * 512], F32))
        atm = ph.enter_context(self.sbufu("atm", [128, 4 * 2048], BF16))
        xrow3 = ph.enter_context(self.sbufu("xrow3", [128, 2 * D], F32))
        y1row = ph.enter_context(self.sbufu("y1row", [128, D], BF16))
        y1tst = ph.enter_context(self.sbufu("y1tst", [128, NDC * 128], BF16))
        p_s0 = ph.enter_context(self.psumu("p_s0", [128, 512], F32))
        p_s1 = ph.enter_context(self.psumu("p_s1", [128, 512], F32))
        p_av0 = ph.enter_context(self.psumu("p_av0", [128, 512], F32))
        p_av1 = ph.enter_context(self.psumu("p_av1", [128, 512], F32))
        p_z0 = ph.enter_context(self.psumu("p_z0", [128, 512], F32))
        p_z1 = ph.enter_context(self.psumu("p_z1", [128, 512], F32))
        p_t0 = ph.enter_context(self.psumu("p_t0", [128, 512], BF16))
        p_t1 = ph.enter_context(self.psumu("p_t1", [128, 512], BF16))
        if True:

            self.new_phase()
            TE, VE, SE, SY, GP = self.TE, self.VE, self.SE, self.SY, self.GP
            QN_OFF, AO_OFF = 0, 12288
            s_dm = S("b3dm")
            GP.incop(s_dm, "dma_start", dmask_t[:], c_dmask[:])
            GP.incop(s_dm, "dma_start", rwt[:], r_w[:])
            for dst_, src_ in [(ltri_t, c_ltri), (erow_t, c_erow),
                               (onesrow_t, c_onesrow)]:
                GP.incop(s_dm, "dma_start", dst_[:], src_[:])
            VE.wait(s_dm)
            s_init = S("b3init")
            breg_s = nc.gpsimd.alloc_register()
            GP.op("reg_mov", breg_s, 128 * E * 3 - 1)
            gp_c = S("gp_c")
            GP.incop(gp_c, "iota", gsent[:], [[0, 1], [0, 1], [0, E], [128, 3]],
                     base=T, channel_multiplier=1)
            GP.wait_force(gp_c)
            GP.incop(s_init, "dma_start", d_gidx[:], gsent[:])
            VE.op("memset", base[0:1, :], 0.0)
            s_scat = S("b3scat")
            TE.wait(s_dm)

            def router_top(rt):
                j = rt % 4
                pv = rt % 2
                b_pav[pv].p(TE)
                for c in range(NDC):
                    TE.incop(t_p, "matmul", pavb[pv][:, 0:E],
                             y1tst[:, c * 128:(c + 1) * 128],
                             rwt[:, c * E:(c + 1) * E],
                             start=(c == 0), stop=(c == NDC - 1),
                             skip_group_check=True)
                b_pav[pv].set_ready(t_p)
                b_y1t.add_free(t_p)
                b_pav[pv].c(VE)
                VE.incop(v_p, "tensor_copy", prb[:, 0:E], pavb[pv][:, 0:E])
                b_pav[pv].add_free(v_p)
                VE.op("tensor_reduce", st["rtm1"][:], prb[:, 0:E], AX.X,
                      ALU.max)
                VE.op("tensor_scalar", prb[:, E:2 * E], prb[:, 0:E],
                      st["rtm1"][:], None, ALU.is_equal)
                VE.op("scalar_tensor_tensor", prb[:, 2 * E:3 * E],
                      prb[:, E:2 * E], -1.0e30, prb[:, 0:E], ALU.mult,
                      ALU.add)
                VE.op("tensor_reduce", st["rtm2"][:], prb[:, 2 * E:3 * E],
                      AX.X, ALU.max)
                VE.op("tensor_scalar", prb[:, 3 * E:4 * E], prb[:, 2 * E:3 * E],
                      st["rtm2"][:], None, ALU.is_equal)
                VE.op("tensor_tensor", dcol[:, j:j + 1], st["rtm1"][:],
                      st["rtm2"][:], ALU.subtract)
                VE.op("tensor_tensor", tmp16[:], prb[:, E:2 * E],
                      prb[:, 3 * E:4 * E], ALU.add)
                VE.op("tensor_copy", indall[:, j * E:(j + 1) * E], tmp16[:])
                for k in range(2):
                    eqs = prb[:, E:2 * E] if k == 0 else prb[:, 3 * E:4 * E]
                    VE.op("tensor_copy",
                          eqall[:, (j * 2 + k) * E:(j * 2 + k + 1) * E], eqs)
                    VE.op("tensor_tensor", tmp16[:], eqs, erow_t[:], ALU.mult)
                    VE.incop(v_p, "tensor_reduce",
                             eall[:, j * 2 + k: j * 2 + k + 1],
                             tmp16[:], AX.X, ALU.add)

            def router_finish(rts):
                CAP = 384
                js = [r_ % 4 for r_ in rts]
                lo, hi = js[0] * E, (js[-1] + 1) * E
                SE.wait(v_p)
                SE.incop(se_p, "activation", wrow[:, js[0]:js[-1] + 1],
                         dcol[:, js[0]:js[-1] + 1], AF.Sigmoid)
                VE.wait(se_p)
                for ji, rt in zip(js, rts):
                    VE.op("tensor_copy", tw[:, rt * 2: rt * 2 + 1],
                          wrow[:, ji:ji + 1])
                    VE.incop(v_p, "tensor_scalar",
                             tw[:, rt * 2 + 1: rt * 2 + 2],
                             wrow[:, ji:ji + 1], -1.0, 1.0, ALU.mult,
                             op1=ALU.add)
                b_pav[0].p(TE)
                TE.wait(v_p)
                TE.incop(t_p, "matmul", pavb[0][:, lo:hi], ltri_t[:],
                         indall[:, lo:hi], start=True, stop=True,
                         skip_group_check=True)
                b_pav[0].set_ready(t_p)
                b_pav[1].p(TE)
                TE.incop(t_p, "matmul", pavb[1][0:1, lo:hi], ones_bf[:, 0:1],
                         indall[:, lo:hi], start=True, stop=True,
                         skip_group_check=True)
                b_pav[1].set_ready(t_p)
                b_pav[1].c(VE)
                for ji in js:
                    VE.op("tensor_copy", bases4[0:1, ji * E:(ji + 1) * E],
                          base[0:1, 0:E])
                    VE.incop(v_p, "tensor_tensor", base[0:1, 0:E],
                             base[0:1, 0:E],
                             pavb[1][0:1, ji * E:(ji + 1) * E], ALU.add)
                b_pav[1].add_free(v_p)
                b_pav[1].p(TE)
                TE.wait(v_p)
                TE.incop(t_p, "matmul", pavb[1][:, lo:hi], onesrow_t[:],
                         bases4[0:1, lo:hi], start=True, stop=True,
                         skip_group_check=True)
                b_pav[1].set_ready(t_p)
                b_pav[0].c(VE)
                VE.op("tensor_scalar", slot4[:, lo:hi], pavb[0][:, lo:hi],
                      -1.0, None, ALU.add)
                b_pav[0].add_free(v_p)
                b_pav[1].c(VE)
                VE.incop(v_p, "tensor_tensor", slot4[:, lo:hi],
                         slot4[:, lo:hi], pavb[1][:, lo:hi], ALU.add)
                b_pav[1].add_free(v_p)
                VE.wait(s_scat)
                for ji, rt in zip(js, rts):
                    for k in range(2):
                        ce = ji * 2 + k
                        eq_ap = eqall[:, ce * E:(ce + 1) * E]
                        VE.op("tensor_tensor", tmp16[:], eq_ap,
                              slot4[:, ji * E:(ji + 1) * E], ALU.mult)
                        VE.op("tensor_reduce", st["lnsum"][:], tmp16[:],
                              AX.X, ALU.add)
                        VE.op("tensor_scalar", st["lnvar"][:], st["lnsum"][:],
                              128.0, None, ALU.is_ge)
                        VE.op("tensor_scalar", st["lntmp"][:], st["lnsum"][:],
                              256.0, None, ALU.is_ge)
                        VE.op("tensor_tensor", st["lnvar"][:], st["lnvar"][:],
                              st["lntmp"][:], ALU.add)
                        VE.op("scalar_tensor_tensor", st["lnsumsq"][:],
                              st["lnvar"][:], -128.0, st["lnsum"][:], ALU.mult,
                              ALU.add)
                        VE.op("tensor_scalar", adrf[:, 0:1], st["lnsumsq"][:],
                              float(E * 3), None, ALU.mult)
                        VE.op("scalar_tensor_tensor", adrf[:, 0:1],
                              eall[:, ce:ce + 1], 3.0, adrf[:, 0:1], ALU.mult,
                              ALU.add)
                        VE.op("tensor_tensor", adrf[:, 0:1], adrf[:, 0:1],
                              st["lnvar"][:], ALU.add)
                        VE.op("tensor_copy", au8[:, ce:ce + 1], adrf[:, 0:1])
                        VE.op("scalar_tensor_tensor", st["lntmp"][:],
                              eall[:, ce:ce + 1], float(CAP), st["lnsum"][:],
                              ALU.mult, ALU.add)
                        VE.op("tensor_scalar", st["lnvar"][:], st["lnsum"][:],
                              float(CAP), None, ALU.is_ge)
                        VE.op("tensor_scalar", st["lnsumsq"][:], st["lntmp"][:],
                              -1.0, float(E * CAP), ALU.mult, op1=ALU.add)
                        VE.op("tensor_tensor", st["lnsumsq"][:],
                              st["lnsumsq"][:], st["lnvar"][:], ALU.mult)
                        VE.op("tensor_tensor", st["lntmp"][:], st["lntmp"][:],
                              st["lnsumsq"][:], ALU.add)
                        VE.incop(v_p, "tensor_copy",
                                 tsrc[:, rt * 2 + k: rt * 2 + k + 1],
                                 st["lntmp"][:])
                for ji, rt in zip(js, rts):
                    GP.wait_force(s_scat)
                    GP.incop(gp_c, "iota", tok32[:], [[0, 1]], base=rt * 128,
                             channel_multiplier=1)
                    GP.wait_force(gp_c)
                    GP.wait(s_init)
                    GP.wait(v_p)
                    for k in range(2):
                        GP.incop(s_scat, "indirect_dma_start",
                                 d_gidx.rearrange("p (c o) -> (p c) o", o=1),
                                 bass.IndirectOffsetOnAxis(
                                     ap=au8[:, ji * 2 + k: ji * 2 + k + 1],
                                     axis=0),
                                 tok32[:, 0:1], None,
                                 bounds_check=breg_s, oob_is_err=False)

            s_qn = S("b3qn")
            b_qn = Buf()
            s_kn = [S("b3kn0"), S("b3kn1")]
            b_kn = [Buf(), Buf()]
            psb = [p_s0, p_s1]
            b_ps = [Buf(), Buf()]
            pavb = [p_av0, p_av1]
            b_pav = [Buf(), Buf()]
            pzb = [p_z0, p_z1]
            b_pz = [Buf(), Buf()]
            ptb = [p_t0, p_t1]
            b_pt = [Buf(), Buf()]
            b_eT = [Buf(), Buf(), Buf()]
            b_rcz = [Buf(), Buf()]
            b_zbc = [Buf(), Buf()]
            b_aofm = Buf()
            s_ow = [S(f"b3ow{i}") for i in range(3)]
            b_ow = [Buf() for _ in range(3)]
            b_oev = [Buf(), Buf()]
            b_atm = [Buf() for _ in range(4)]
            b_xr = [Buf(), Buf()]
            s_xr = [S("b3xr0"), S("b3xr1")]
            b_y1r = Buf()
            b_y1t = Buf()
            s_ybs = [S("ybs0"), S("ybs1"), S("ybs2"), S("ybs3")]
            pend = []
            fin_pend = []

            def flush_pend():
                for (s4, ev, rt) in pend:
                    b_y1t.p(VE)
                    for g4 in range(4):
                        tp = g4 % 2
                        b_pt[tp].p(TE)
                        TE.wait_abs(*ev)
                        for jj in range(4):
                            c = g4 * 4 + jj
                            TE.incop(t_p, "transpose",
                                     ptb[tp][:, jj * 128:(jj + 1) * 128],
                                     atm[:, s4 * 2048 + c * 128:
                                         s4 * 2048 + (c + 1) * 128],
                                     ident_bf[:])
                        b_pt[tp].set_ready(t_p)
                        if g4 == 3:
                            b_atm[s4].add_free(t_p)
                        b_pt[tp].c(VE)
                        VE.incop(v_p, "tensor_copy",
                                 y1tst[:, g4 * 512:(g4 + 1) * 512], ptb[tp][:, :])
                        b_pt[tp].add_free(v_p)
                    b_y1t.set_ready(v_p)
                    b_y1t.c(TE)
                    router_top(rt)
                    fin_pend.append(rt)
                pend.clear()

            def issue_ow(c):
                sl = c % 3
                b_ow[sl].p(GP)
                GP.incop(s_ow[sl], "dma_start",
                         owt[:, sl * 2048:(sl + 1) * 2048], o_w[c])
                b_ow[sl].set_ready(s_ow[sl])

            ln_pend = []

            def emit_ln1(qb_, s_):
                rt = qb_ * 4 + s_
                xp = rt % 2
                xr = xrow3[:, xp * D:(xp + 1) * D]
                b_xr[xp].p(SY)
                SY.incop(s_xr[xp], "dma_start", xr,
                         x[rt * 128:(rt + 1) * 128, :])
                b_xr[xp].set_ready(s_xr[xp])
                b_xr[xp].c(VE)
                VE.op("tensor_tensor", xr,
                      atm[:, s_ * 2048:(s_ + 1) * 2048], xr, ALU.add)
                b_y1r.p(VE)
                VE.op("tensor_scalar", y1row[:], xr, 0.0, None,
                      ALU.add, op1=ALU.add, accum_out=st["lnsum"][:])
                VE.op("tensor_tensor", y1row[:], xr, xr, ALU.mult)
                VE.op("tensor_scalar", y1row[:], y1row[:], 0.0, None,
                      ALU.add, op1=ALU.add, accum_out=st["lnsumsq"][:])
                VE.op("tensor_scalar", st["lnmu"][:], st["lnsum"][:],
                      1.0 / D, None, ALU.mult)
                VE.op("tensor_scalar", st["lnvar"][:], st["lnsumsq"][:],
                      1.0 / D, None, ALU.mult)
                VE.op("tensor_tensor", st["lntmp"][:], st["lnmu"][:],
                      st["lnmu"][:], ALU.mult)
                VE.incop(v_p, "tensor_tensor", st["lnvar"][:],
                         st["lnvar"][:], st["lntmp"][:], ALU.subtract)
                b_y1r.set_ready(v_p)
                b_y1r.add_free(v_p)
                SE.wait(v_p)
                SE.op("activation", st["lnrstd"][:], st["lnvar"][:], AF.Ln,
                      bias=eps_ln[:])
                SE.incop(se_p, "activation", st["lnrstd"][:],
                         st["lnrstd"][:], AF.Exp, scale=-0.5)
                VE.wait(se_p)
                VE.incop(v_p, "tensor_scalar",
                         atm[:, s_ * 2048:(s_ + 1) * 2048], xr,
                         st["lnmu"][:], st["lnrstd"][:], ALU.subtract,
                         op1=ALU.mult)
                b_xr[xp].add_free(v_p)
                SY.wait_abs(v_p, v_p.v)
                SY.incop(s_ybs[s_], "dma_start",
                         d_y1bf[rt * 128:(rt + 1) * 128, :],
                         atm[:, s_ * 2048:(s_ + 1) * 2048])
                b_atm[s_].add_free(s_ybs[s_])
                pend.append((s_, (v_p, v_p.v), rt))

            NQB = T // 512

            def issue_qT(qb_):
                b_qn.p(SE)
                SE.incop(s_qn, "dma_start", P2[:, QN_OFF:QN_OFF + NQBF * 512],
                         d_qT[qb_])
                b_qn.set_ready(s_qn)

            def issue_kn(qb_, h, rec):
                L_ = (qb_ + 1) * 512
                kp_ = h % 2
                b_kn[kp_].p(rec)
                rec.incop(s_kn[kp_], "dma_start",
                          kn_sb[:, kp_ * 2048: kp_ * 2048 + L_],
                          d_kn[h * 128:(h + 1) * 128, 0:L_])
                b_kn[kp_].set_ready(s_kn[kp_])

            issue_qT(0)
            issue_kn(0, 0, SY)
            issue_kn(0, 1, SY)
            for qb in range(NQB):
                L = (qb + 1) * 512
                nkt = 4 * (qb + 1)
                for c4 in range(3):
                    issue_ow(c4)
                for h in range(H):
                    if ln_pend and h < 4:
                        emit_ln1(*ln_pend.pop(0))
                    kp = h % 2
                    hp = h % 2
                    b_pav[hp].p(TE)
                    b_pz[hp].p(TE)
                    if h == 0:
                        b_aofm.p(VE)
                    avq = []

                    def emit_av(kt):
                        es = kt % 3
                        b_eT[es].c(TE)
                        TE.incop(t_p, "matmul", pavb[hp][:],
                                 P1[:, (kt * H + h) * 128:(kt * H + h + 1) * 128],
                                 eT[:, es * 512:(es + 1) * 512],
                                 start=(kt == 0), stop=(kt == nkt - 1),
                                 skip_group_check=True)
                        TE.incop(t_p, "matmul", pzb[hp][0:1, :],
                                 ones_bf[:, 0:1],
                                 eT[:, es * 512:(es + 1) * 512],
                                 start=(kt == 0), stop=(kt == nkt - 1),
                                 skip_group_check=True)
                        b_eT[es].add_free(t_p)

                    for kt in range(nkt):
                        sp = kt % 2
                        es = kt % 3
                        b_ps[sp].p(TE)
                        b_qn.c(TE)
                        b_kn[kp].c(TE)
                        TE.incop(t_p, "matmul", psb[sp][:],
                                 kn_sb[:, kp * 2048 + kt * 128:
                                       kp * 2048 + (kt + 1) * 128],
                                 P2[:, QN_OFF + h * 512: QN_OFF + (h + 1) * 512],
                                 start=True, stop=False, skip_group_check=True)
                        hb = (h % 2) * 64
                        TE.incop(t_p, "matmul", psb[sp][:],
                                 krope[hb:hb + 64, kt * 128:(kt + 1) * 128],
                                 P2[hb:hb + 64, (16 + h // 2) * 512:
                                    (16 + h // 2) * 512 + 512],
                                 start=False, stop=True, skip_group_check=True)
                        b_ps[sp].set_ready(t_p)
                        if kt == nkt - 1:
                            b_kn[kp].add_free(t_p)
                            if h == H - 1:
                                b_qn.add_free(t_p)
                        b_ps[sp].c(SE)
                        b_eT[es].p(SE)
                        SE.incop(se_p, "activation",
                                 eT[:, es * 512:(es + 1) * 512],
                                 psb[sp][:], AF.Exp, bias=mneg_sb[:, h:h + 1])
                        b_ps[sp].add_free(se_p)
                        if kt >= nkt - 4:
                            jloc = kt - (nkt - 4)
                            VE.wait(se_p)
                            b_eT[es].p(VE)
                            VE.incop(v_p, "tensor_tensor",
                                     eT[:, es * 512:(es + 1) * 512],
                                     eT[:, es * 512:(es + 1) * 512],
                                     dmask_t[:, jloc * 512:(jloc + 1) * 512],
                                     ALU.mult)
                            b_eT[es].set_ready(v_p)
                        else:
                            b_eT[es].set_ready(se_p)
                        avq.append(kt)
                        if len(avq) > 2:
                            emit_av(avq.pop(0))
                    if h + 2 < H:
                        issue_kn(qb, h + 2, SY)
                    while avq:
                        emit_av(avq.pop(0))
                    b_pav[hp].set_ready(t_p)
                    b_pz[hp].set_ready(t_p)
                    b_pz[hp].c(SE)
                    b_rcz[hp].p(SE)
                    SE.op("activation", rczf[0:1, :], pzb[hp][0:1, :], AF.Ln)
                    SE.incop(se_p, "activation",
                             rcz[0:1, hp * 512:(hp + 1) * 512],
                             rczf[0:1, :], AF.Exp, scale=-1.0)
                    b_rcz[hp].set_ready(se_p)
                    b_rcz[hp].c(TE)
                    TE.wait(se_p)
                    TE.incop(t_p, "matmul", pzb[hp][:], ones_bf[0:1, 0:128],
                             rcz[0:1, hp * 512:(hp + 1) * 512],
                             start=True, stop=True, skip_group_check=True)
                    b_rcz[hp].add_free(t_p)
                    b_pz[hp].set_ready(t_p)
                    b_pz[hp].c(SE)
                    b_zbc[hp].p(SE)
                    SE.incop(se_p, "activation",
                             zbc[:, hp * 512:(hp + 1) * 512],
                             pzb[hp][:], AF.Copy)
                    b_pz[hp].add_free(se_p)
                    b_zbc[hp].set_ready(se_p)
                    b_pav[hp].c(VE)
                    b_zbc[hp].c(VE)
                    VE.incop(v_p, "tensor_tensor",
                             P2[:, AO_OFF + h * 512: AO_OFF + (h + 1) * 512],
                             pavb[hp][:], zbc[:, hp * 512:(hp + 1) * 512],
                             ALU.mult)
                    b_pav[hp].add_free(v_p)
                    b_zbc[hp].add_free(v_p)
                    if h == H - 1:
                        b_aofm.set_ready(v_p)
                if qb + 1 < NQB:
                    issue_qT(qb + 1)
                    issue_kn(qb + 1, 0, SE)
                    issue_kn(qb + 1, 1, SE)
                flush_pend()
                b_aofm.c(TE)
                for s4x in range(4):
                    b_atm[s4x].p(VE)
                    b_atm[s4x].free = []
                for c in range(NDC):
                    sl = c % 3
                    op_ = c % 2
                    b_ow[sl].c(TE)
                    b_ps[op_].p(TE)
                    for hv in range(NDC):
                        TE.incop(t_p, "matmul", psb[op_][:],
                                 owt[:, sl * 2048 + hv * 128:
                                     sl * 2048 + (hv + 1) * 128],
                                 P2[:, AO_OFF + hv * 512: AO_OFF + (hv + 1) * 512],
                                 start=(hv == 0), stop=(hv == NDC - 1))
                    b_ps[op_].set_ready(t_p)
                    b_ow[sl].add_free(t_p)
                    if c + 3 < NDC:
                        issue_ow(c + 3)
                    if c == NDC - 1:
                        b_aofm.add_free(t_p)
                    b_ps[op_].c(SE)
                    b_oev[op_].p(SE)
                    SE.incop(se_p, "activation",
                             oevac[:, op_ * 512:(op_ + 1) * 512], psb[op_][:],
                             AF.Copy)
                    b_ps[op_].add_free(se_p)
                    b_oev[op_].set_ready(se_p)
                    b_oev[op_].c(TE)
                    b_pz[op_].p(TE)
                    for s_ in range(4):
                        TE.incop(t_p, "transpose",
                                 pzb[op_][:, s_ * 128:(s_ + 1) * 128],
                                 oevac[:, op_ * 512 + s_ * 128:
                                       op_ * 512 + (s_ + 1) * 128],
                                 ident_f32[:])
                    b_pz[op_].set_ready(t_p)
                    b_oev[op_].add_free(t_p)
                    b_pz[op_].c(VE)
                    for s_ in range(4):
                        VE.incop(v_p, "tensor_copy",
                                 atm[:, s_ * 2048 + c * 128:
                                     s_ * 2048 + (c + 1) * 128],
                                 pzb[op_][:, s_ * 128:(s_ + 1) * 128])
                    b_pz[op_].add_free(v_p)
                if fin_pend:
                    router_finish(fin_pend)
                    fin_pend = []
                ln_pend.extend((qb, s_) for s_ in range(4))
            while ln_pend:
                emit_ln1(*ln_pend.pop(0))
            flush_pend()
            if fin_pend:
                router_finish(fin_pend)
                fin_pend = []
            self.emit_phase()
            ph.close()
        mid_stack.close()
        CAP = 384
        NBLK = E * 3
        ph = contextlib.ExitStack()
        gidx_sb = ph.enter_context(self.sbufu("gidx_sb", [128, NBLK], mybir.dt.uint32))
        gwt = ph.enter_context(self.sbufu("gwt", [128, 3 * NDC * 128], BF16))
        uwt = ph.enter_context(self.sbufu("uwt", [128, 3 * NDC * 128], BF16))
        dwq = ph.enter_context(self.sbufu("dwq", [128, 4 * NFF * 512], BF16))
        sgb = ph.enter_context(self.sbufu("sgb", [128, 2 * 512], BF16))
        uscb = ph.enter_context(self.sbufu("uscb", [128, 2 * 512], BF16))
        stA = ph.enter_context(self.sbufu("stA", [128, 2 * D], BF16))
        stage = stA
        stB = ph.enter_context(self.sbufu("stB", [128, 2 * D], BF16))
        accrow = ph.enter_context(self.sbufu("accrow", [128, D], F32))
        y1r4 = ph.enter_context(self.sbufu("y1r4", [128, D], BF16))
        y2row = ph.enter_context(self.sbufu("y2row", [128, D], F32))
        p0 = ph.enter_context(self.psumu("p0", [128, 512], F32))
        p1 = ph.enter_context(self.psumu("p1", [128, 512], F32))
        p2q = ph.enter_context(self.psumu("p2q", [128, 512], F32))
        p3 = ph.enter_context(self.psumu("p3", [128, 512], F32))
        p4 = ph.enter_context(self.psumu("p4", [128, 512], F32))
        p5 = ph.enter_context(self.psumu("p5", [128, 512], F32))
        p6 = ph.enter_context(self.psumu("p6", [128, 512], BF16))
        p7 = ph.enter_context(self.psumu("p7", [128, 512], BF16))
        if True:
            self.new_phase()
            TE, VE, SE, SY, GP = self.TE, self.VE, self.SE, self.SY, self.GP
            XG_OFF = 0
            XGT_OFF = 8 * 2048
            b_p = [Buf() for _ in range(8)]
            pb = [p0, p1, p2q, p3, p4, p5, p6, p7]

            s_ez = S("b4ez")
            VE.incop(v_p, "memset", stage[:, 0:D], 0.0)
            SE.wait(v_p)
            SE.incop(s_ez, "dma_start", d_expout[E * CAP:E * CAP + 128, :],
                     stage[:, 0:D])
            breg_g = nc.gpsimd.alloc_register()
            breg_e = nc.gpsimd.alloc_register()
            GP.op("reg_mov", breg_g, T - 1)
            GP.op("reg_mov", breg_e, E * CAP + 127)

            s_gl = S("b4gl")
            SY.wait(s_scat)
            SY.incop(s_gl, "dma_start", gidx_sb[:], d_gidx[:])

            b_xg = [Buf() for _ in range(8)]
            s_xg = [S(f"b4xg{i}") for i in range(8)]
            b_xgT = [Buf(), Buf()]
            b_gw = [Buf() for _ in range(3)]
            b_uw = [Buf() for _ in range(3)]
            s_gw = [S(f"b4g{i}") for i in range(3)]
            s_uw = [S(f"b4u{i}") for i in range(3)]
            b_dw = [Buf() for _ in range(4)]
            s_dw = [S(f"b4d{i}") for i in range(4)]
            b_stg = [Buf(), Buf()]
            s_st = [S("b4s0"), S("b4s1")]
            b_stg[0].add_free(s_ez)
            b_h = [Buf(), Buf()]
            b_sgb = [Buf(), Buf()]

            def issue_gu(idx):
                if idx >= E * NFF:
                    return
                e_, f_ = idx // NFF, idx % NFF
                wp_ = idx % 3
                b_gw[wp_].p(GP)
                GP.incop(s_gw[wp_], "dma_start",
                         gwt[:, wp_ * NDC * 128:(wp_ + 1) * NDC * 128],
                         gate_w[e_, f_])
                b_gw[wp_].set_ready(s_gw[wp_])
                b_uw[wp_].p(GP)
                GP.incop(s_uw[wp_], "dma_start",
                         uwt[:, wp_ * NDC * 128:(wp_ + 1) * NDC * 128],
                         up_w[e_, f_])
                b_uw[wp_].set_ready(s_uw[wp_])

            def issue_xg(idx):
                if idx >= E * 3:
                    return
                e_, b_ = idx // 3, idx % 3
                sl = idx % 8
                b_xg[sl].p(GP)
                GP.wait(s_gl)
                GP.incop(s_xg[sl], "indirect_dma_start",
                         P1[:, XG_OFF + sl * 2048: XG_OFF + (sl + 1) * 2048],
                         None, d_y1bf[:],
                         bass.IndirectOffsetOnAxis(
                             ap=gidx_sb[:, e_ * 3 + b_: e_ * 3 + b_ + 1],
                             axis=0),
                         bounds_check=breg_g, oob_is_err=False)
                b_xg[sl].set_ready(s_xg[sl])

            for idx0 in range(3):
                issue_gu(idx0)
            for idx0 in range(6):
                issue_xg(idx0)
            dwi = 0
            gi = 0
            for e in range(E):
                epar = e % 2
                dsl = {}
                for ds in range(4):
                    sl = dwi % 4
                    dwi += 1
                    dsl[ds] = sl
                    b_dw[sl].p(GP)
                    GP.incop(s_dw[sl], "dma_start",
                             dwq[:, sl * NFF * 512:(sl + 1) * NFF * 512],
                             down_w[e, ds])
                    b_dw[sl].set_ready(s_dw[sl])
                b_xgT[epar].p(TE)
                for b_ in range(3):
                    sl = (e * 3 + b_) % 8
                    b_xg[sl].c(TE)
                    for c in range(NDC):
                        tp_ = c % 2
                        b_p[6 + tp_].p(TE)
                        TE.incop(t_p, "transpose",
                                 pb[6 + tp_][:, 0:128],
                                 P1[:, XG_OFF + sl * 2048 + c * 128:
                                    XG_OFF + sl * 2048 + (c + 1) * 128],
                                 ident_bf[:])
                        b_p[6 + tp_].set_ready(t_p)
                        if c == NDC - 1:
                            b_xg[sl].add_free(t_p)
                        b_p[6 + tp_].c(SE)
                        SE.incop(se_p, "activation",
                                 P1[:, XGT_OFF + epar * 6144 + c * 384 + b_ * 128:
                                    XGT_OFF + epar * 6144 + c * 384 + (b_ + 1) * 128],
                                 pb[6 + tp_][:, 0:128], AF.Copy)
                        b_p[6 + tp_].add_free(se_p)
                for b_ in range(3):
                    issue_xg((e + 2) * 3 + b_)
                xgt_ev = (se_p, se_p.v)
                b_xgT[epar].set_ready(se_p)
                b_h[epar].p(VE)
                for f in range(NFF):
                    wp = gi % 3
                    gp_ = f % 2
                    b_gw[wp].c(TE)
                    b_uw[wp].c(TE)
                    TE.wait_abs(*xgt_ev)
                    b_p[gp_].p(TE)
                    for c in range(NDC):
                        TE.incop(t_p, "matmul", pb[gp_][:, 0:CAP],
                                 gwt[:, wp * NDC * 128 + c * 128:
                                     wp * NDC * 128 + (c + 1) * 128],
                                 P1[:, XGT_OFF + epar * 6144 + c * 384:
                                    XGT_OFF + epar * 6144 + (c + 1) * 384],
                                 start=(c == 0), stop=(c == NDC - 1))
                    b_p[gp_].set_ready(t_p)
                    b_gw[wp].add_free(t_p)
                    b_p[2 + gp_].p(TE)
                    for c in range(NDC):
                        TE.incop(t_p, "matmul", pb[2 + gp_][:, 0:CAP],
                                 uwt[:, wp * NDC * 128 + c * 128:
                                     wp * NDC * 128 + (c + 1) * 128],
                                 P1[:, XGT_OFF + epar * 6144 + c * 384:
                                    XGT_OFF + epar * 6144 + (c + 1) * 384],
                                 start=(c == 0), stop=(c == NDC - 1))
                    b_p[2 + gp_].set_ready(t_p)
                    b_uw[wp].add_free(t_p)
                    issue_gu(gi + 3)
                    gi += 1
                    if f == NFF - 1:
                        b_xgT[epar].add_free(t_p)
                    b_p[gp_].c(SE)
                    b_sgb[gp_].p(SE)
                    SE.incop(se_p, "activation",
                             sgb[:, gp_ * 512: gp_ * 512 + CAP],
                             pb[gp_][:, 0:CAP], AF.Silu)
                    b_p[gp_].add_free(se_p)
                    sgb_ev = (se_p, se_p.v)
                    b_p[2 + gp_].c(VE)
                    VE.incop(v_p, "tensor_copy",
                             uscb[:, gp_ * 512: gp_ * 512 + CAP],
                             pb[2 + gp_][:, 0:CAP])
                    b_p[2 + gp_].add_free(v_p)
                    VE.wait_abs(*sgb_ev)
                    VE.incop(v_p, "tensor_tensor",
                             P2[:, (epar * NFF + f) * CAP:
                                (epar * NFF + f + 1) * CAP],
                             sgb[:, gp_ * 512: gp_ * 512 + CAP],
                             uscb[:, gp_ * 512: gp_ * 512 + CAP], ALU.mult)
                    b_sgb[gp_].add_free(v_p)
                b_h[epar].set_ready(v_p)
                b_h[epar].c(TE)
                for m in range(3):
                    spar = m % 2
                    for ds in range(4):
                        dpar = 4 + (ds % 2)
                        sl = dsl[ds]
                        b_dw[sl].c(TE)
                        b_p[dpar].p(TE)
                        for f in range(NFF):
                            TE.incop(t_p, "matmul", pb[dpar][:],
                                     P2[:, (epar * NFF + f) * CAP + m * 128:
                                        (epar * NFF + f) * CAP + (m + 1) * 128],
                                     dwq[:, sl * NFF * 512 + f * 512:
                                         sl * NFF * 512 + (f + 1) * 512],
                                     start=(f == 0), stop=(f == NFF - 1))
                        b_p[dpar].set_ready(t_p)
                        if m == 2:
                            b_dw[sl].add_free(t_p)
                        b_p[dpar].c(VE)
                        if ds == 0:
                            b_stg[spar].p(VE)
                        VE.incop(v_p, "tensor_copy",
                                 stage[:, spar * D + ds * 512:
                                       spar * D + (ds + 1) * 512], pb[dpar][:])
                        b_p[dpar].add_free(v_p)
                    b_stg[spar].set_ready(v_p)
                    b_stg[spar].c(SY)
                    SY.incop(s_st[spar], "dma_start",
                             d_expout[e * CAP + m * 128: e * CAP + (m + 1) * 128, :],
                             stage[:, spar * D:(spar + 1) * D])
                    b_stg[spar].add_free(s_st[spar])
                b_h[epar].add_free(t_p)

            GP.wait(s_st[0])
            GP.wait(s_st[1])
            GP.wait(s_ez)
            b_gA = [Buf(), Buf()]
            b_gB = [Buf(), Buf()]
            s_gA = [S("b4ga0"), S("b4ga1")]
            s_gB = [S("b4gb0"), S("b4gb1")]
            b_y14 = [Buf(), Buf()]
            s_y14 = [S("b4y140"), S("b4y141")]
            b_acr = Buf()
            b_y2 = Buf()
            for rt in range(NRT):
                cp = rt % 2
                b_gA[cp].p(GP)
                GP.incop(s_gA[cp], "indirect_dma_start",
                         stA[:, cp * D:(cp + 1) * D], None, d_expout[:],
                         bass.IndirectOffsetOnAxis(
                             ap=tsrc[:, rt * 2: rt * 2 + 1], axis=0),
                         bounds_check=breg_e, oob_is_err=False)
                b_gA[cp].set_ready(s_gA[cp])
                b_gB[cp].p(GP)
                GP.incop(s_gB[cp], "indirect_dma_start",
                         stB[:, cp * D:(cp + 1) * D], None, d_expout[:],
                         bass.IndirectOffsetOnAxis(
                             ap=tsrc[:, rt * 2 + 1: rt * 2 + 2], axis=0),
                         bounds_check=breg_e, oob_is_err=False)
                b_gB[cp].set_ready(s_gB[cp])
                b_y14[0].p(SY)
                SY.incop(s_y14[0], "dma_start", y1r4[:],
                         d_y1bf[rt * 128:(rt + 1) * 128, :])
                b_y14[0].set_ready(s_y14[0])
                b_gA[cp].c(VE)
                b_y14[0].c(VE)
                b_acr.p(VE)
                VE.op("scalar_tensor_tensor", accrow[:],
                      stA[:, cp * D:(cp + 1) * D], tw[:, rt * 2: rt * 2 + 1],
                      y1r4[:], ALU.mult, ALU.add)
                b_y14[0].add_free(v_p)
                b_gB[cp].c(VE)
                VE.incop(v_p, "scalar_tensor_tensor", accrow[:],
                         stB[:, cp * D:(cp + 1) * D],
                         tw[:, rt * 2 + 1: rt * 2 + 2], accrow[:],
                         ALU.mult, ALU.add)
                b_gA[cp].add_free(v_p)
                b_gB[cp].add_free(v_p)
                SE.wait(v_p)
                b_y2.p(SE)
                SE.op("activation", y2row[:], accrow[:], AF.Copy,
                      accum_out=st["lnsum"][:])
                SE.op("activation", y2row[:], accrow[:], AF.Square,
                      accum_out=st["lnsumsq"][:])
                VE.wait(se_p)
                VE.op("tensor_scalar", st["lnmu"][:], st["lnsum"][:],
                      1.0 / D, None, ALU.mult)
                VE.op("tensor_scalar", st["lnvar"][:], st["lnsumsq"][:],
                      1.0 / D, None, ALU.mult)
                VE.op("tensor_tensor", st["lntmp"][:], st["lnmu"][:],
                      st["lnmu"][:], ALU.mult)
                VE.incop(v_p, "tensor_tensor", st["lnvar"][:], st["lnvar"][:],
                         st["lntmp"][:], ALU.subtract)
                SE.wait(v_p)
                SE.op("activation", st["lnrstd"][:], st["lnvar"][:], AF.Ln,
                      bias=eps_ln[:])
                SE.incop(se_p, "activation", st["lnrstd"][:], st["lnrstd"][:],
                         AF.Exp, scale=-0.5)
                VE.wait(se_p)
                VE.incop(v_p, "tensor_scalar", y2row[:], accrow[:],
                         st["lnmu"][:], st["lnrstd"][:], ALU.subtract,
                         op1=ALU.mult)
                b_acr.add_free(v_p)
                b_y2.set_ready(v_p)
                b_y2.c(SY)
                SY.incop(s_out, "dma_start", out[rt * 128:(rt + 1) * 128, :],
                         y2row[:])
                b_y2.add_free(s_out)
            SY.wait(s_out)
            self.emit_phase()
            ph.close()


        return self.nc


def kernel(**inputs):
    T = 2048
    B = inputs["L0_attn_x"].shape[0]
    prep = host_prep(inputs, T)
    b = Builder(T)
    nc = b.build()
    in_maps = []
    for i in range(B):
        m = dict(prep)
        m["x"] = np.ascontiguousarray(inputs["L0_attn_x"][i])
        in_maps.append(m)
    res = run_bass_kernel_spmd(nc, in_maps, core_ids=list(range(B)))
    return np.stack([np.asarray(res.results[i]["out"]) for i in range(B)], axis=0)


# version 27 (final)
# speedup vs baseline: 1.2025x; 1.2025x over previous
import contextlib
import numpy as np
import ml_dtypes
import concourse.bass as bass
import concourse.mybir as mybir
from concourse.alu_op_type import AluOpType as ALU
from concourse.bass_utils import run_bass_kernel_spmd

F32 = mybir.dt.float32
BF16 = mybir.dt.bfloat16
AF = mybir.ActivationFunctionType
AX = mybir.AxisListType

H = 16
KV_LORA = 512
Q_LORA = 1536
D = 2048
E = 16
FF = 768
SCALE = float(192.0 ** (-0.5))
RMS_EPS = 1e-6
LN_EPS = 1e-5
NEG = -1.0e30
ROPE_THETA = 500000.0

NQAC = Q_LORA // 128
NQBF = 24
NKVC = KV_LORA // 128
NKVF = 32
NDC = D // 128
NFF = FF // 128


def _deint():
    return np.concatenate([np.arange(0, 64, 2), np.arange(1, 64, 2)])


def host_prep(inputs, T):
    d = {}
    cols = np.arange(H * 192).reshape(H, 192)
    qbp = np.concatenate([cols[:, :128].reshape(-1),
                          cols[:, 128:][:, _deint()].reshape(-1)])
    cols = np.arange(H * 256).reshape(H, 256)
    kvbp = np.concatenate([cols[:, :128].reshape(-1), cols[:, 128:].reshape(-1)])
    kvap = np.concatenate([np.arange(KV_LORA), KV_LORA + _deint()])

    qa = inputs["L0_attn_q_a_w"]
    qb = np.ascontiguousarray(inputs["L0_attn_q_b_w"][:, qbp])
    kva = np.ascontiguousarray(inputs["L0_attn_kv_a_w"][:, kvap])
    kvb = np.ascontiguousarray(inputs["L0_attn_kv_b_w"][:, kvbp])
    ow = inputs["L0_attn_o_w"]

    d["qa_w"] = np.ascontiguousarray(
        qa.reshape(NDC, 128, NQAC, 128).transpose(2, 1, 0, 3)
        .reshape(NQAC, 128, NDC * 128))
    d["qb_w"] = np.ascontiguousarray(
        qb.reshape(NQAC, 128, NQBF, 128).transpose(2, 1, 0, 3)
        .reshape(NQBF, 128, NQAC * 128))
    d["kva_w"] = np.ascontiguousarray(
        kva.reshape(NDC, 128, 576).transpose(1, 0, 2).reshape(128, NDC * 576))
    d["kvb_w"] = np.ascontiguousarray(
        kvb.reshape(NKVC, 128, H * 256).transpose(1, 0, 2).reshape(128, NKVC * H * 256))
    d["o_w"] = np.ascontiguousarray(
        ow.reshape(NDC, 128, NDC, 128).transpose(2, 1, 0, 3).reshape(NDC, 128, NDC * 128))
    d["r_w"] = np.ascontiguousarray(
        inputs["L0_ffn_router_w"].reshape(NDC, 128, E).transpose(1, 0, 2)
        .reshape(128, NDC * E))
    d["gate_w"] = np.ascontiguousarray(
        inputs["L0_ffn_gate_w"].reshape(E, NDC, 128, NFF, 128)
        .transpose(0, 3, 2, 1, 4).reshape(E, NFF, 128, NDC * 128))
    d["up_w"] = np.ascontiguousarray(
        inputs["L0_ffn_up_w"].reshape(E, NDC, 128, NFF, 128)
        .transpose(0, 3, 2, 1, 4).reshape(E, NFF, 128, NDC * 128))
    d["down_w"] = np.ascontiguousarray(
        inputs["L0_ffn_down_w"].reshape(E, NFF, 128, 4, 512)
        .transpose(0, 3, 2, 1, 4).reshape(E, 4, 128, NFF * 512))
    d["qa_ln"] = np.ascontiguousarray(inputs["L0_attn_q_a_ln_w"].reshape(NQAC, 128))
    d["kva_ln"] = np.ascontiguousarray(inputs["L0_attn_kv_a_ln_w"].reshape(NKVC, 128))

    inv = 1.0 / ROPE_THETA ** (np.arange(0, 64, 2, dtype=np.float32) / 64)
    fr = np.outer(np.arange(T, dtype=np.float32), inv)
    emb = np.concatenate([fr, fr], axis=-1)
    cos = np.cos(emb).astype(np.float32).T
    sin = np.sin(emb).astype(np.float32).T
    d["c_cos"] = np.ascontiguousarray(np.concatenate([cos, cos], 0))
    d["c_sin"] = np.ascontiguousarray(np.concatenate([sin, sin], 0))
    P = np.zeros((128, 128), np.float32)
    for b0 in (0, 64):
        for i in range(32):
            P[b0 + i, b0 + i + 32] = -1.0
            P[b0 + i + 32, b0 + i] = 1.0
    d["c_rot"] = np.ascontiguousarray(P.T)
    M = np.zeros((64, 128), np.float32)
    for m in range(128):
        M[m % 64, m] = 1.0
    d["c_dup"] = M
    dm = np.zeros((128, 4 * 512), np.float32)
    for j in range(4):
        for k in range(128):
            dm[k, j * 512 + j * 128 + k:(j + 1) * 512] = 1.0
    d["c_dmask"] = dm.astype(ml_dtypes.bfloat16)
    d["c_ident_bf"] = np.eye(128, dtype=ml_dtypes.bfloat16)
    d["c_ident_f32"] = np.eye(128, dtype=np.float32)
    d["c_ones_bf"] = np.ones((128, 128), ml_dtypes.bfloat16)
    lt = np.triu(np.ones((128, 128), np.float32))
    d["c_ltri"] = lt.astype(ml_dtypes.bfloat16)
    d["c_erow"] = np.broadcast_to(np.arange(E, dtype=np.float32), (128, E)).copy()
    d["c_onesrow"] = np.ones((1, 128), np.float32)
    return d


class Sem:
    __slots__ = ("h", "name", "v", "owner", "dma")

    def __init__(self, h, name):
        self.h = h
        self.name = name
        self.v = 0
        self.owner = None
        self.dma = False


class Rec:
    def __init__(self, name, chain=None):
        self.name = name
        self.ops = []
        self.maxwait = {}
        self.chain = chain

    def wait_abs(self, sem, v):
        if v <= 0:
            return
        if sem.owner == self.name and not sem.dma:
            return
        if self.maxwait.get(id(sem), -1) >= v:
            return
        self.maxwait[id(sem)] = v
        self.ops.append(("w", sem.h, v))

    def wait(self, sem):
        self.wait_abs(sem, sem.v)

    def wait_force(self, sem, v=None):
        v = sem.v if v is None else v
        if v > 0 and self.maxwait.get(id(sem), -1) < v:
            self.maxwait[id(sem)] = v
            self.ops.append(("w", sem.h, v))

    def op(self, meth, *a, inc=None, **k):
        if self.chain is not None and "dma" not in meth and meth != "wait_ge":
            assert inc is None
            self.incop(self.chain, meth, *a, **k)
            return
        self.ops.append(("o", meth, a, k, inc))

    def incop(self, sem, meth, *a, **k):
        if "dma" in meth:
            sem.dma = True
            n = 16
        else:
            n = 1
            assert self.chain is None or sem is self.chain
        if sem.owner is None:
            sem.owner = self.name
        if self.chain is not None and "dma" not in meth and self.chain.v > 0:
            self.ops.append(("w", self.chain.h, self.chain.v))
        sem.v += n
        self.ops.append(("o", meth, a, k, (sem.h, n)))

    def replay(self, eng):
        for o in self.ops:
            if o[0] == "w":
                eng.wait_ge(o[1], o[2])
            else:
                _, meth, a, k, inc = o
                try:
                    r = getattr(eng, meth)(*a, **k)
                except Exception:
                    import sys
                    print(f"REPLAY FAIL {self.name} {meth} args={a} kw={k}",
                          file=sys.stderr)
                    raise
                if inc is not None:
                    r.then_inc(inc[0], inc[1])


class Buf:

    def __init__(self):
        self.ready = None
        self.free = []

    def p(self, R):
        for ev in self.free:
            R.wait_abs(*ev)

    def set_ready(self, sem):
        self.ready = (sem, sem.v)
        self.free = []

    def c(self, R):
        if self.ready:
            R.wait_abs(*self.ready)

    def add_free(self, sem):
        self.free.append((sem, sem.v))


class Builder:
    def __init__(self, T, dbg=False, max_phases=99):
        self.T = T
        self.dbg = dbg
        self.max_phases = max_phases
        self.phase_no = 0
        self.nc = bass.Bass("TRN2", target_bir_lowering=False)
        self.sems = {}
        self.stack = contextlib.ExitStack()
        self.NTT = T // 512
        self.NRT = T // 128

    def S(self, name):
        if name not in self.sems:
            self.sems[name] = Sem(self.nc.alloc_semaphore(name=name), name)
        return self.sems[name]

    def _u(self, n):
        self._uc = getattr(self, "_uc", 0) + 1
        return f"{n}_{self._uc}"

    def sbufu(self, name, shape, dt):
        return self.nc.sbuf_tensor(self._u(name), shape, dt)

    def psumu(self, name, shape, dt):
        return self.nc.psum_tensor(self._u(name), shape, dt)

    def new_phase(self):
        self.TE = Rec("TE")
        self.VE = Rec("VE", chain=self.S("v_p"))
        self.SE = Rec("SE", chain=self.S("se_p"))
        self.SY = Rec("SY")
        self.GP = Rec("GP")

    def emit_phase(self):
        self.phase_no += 1
        if self.phase_no > self.max_phases:
            return
        for rec in (self.SY, self.GP):
            for sem in self.sems.values():
                if sem.dma and sem.v > 0:
                    rec.wait_abs(sem, sem.v)
        with self.nc.Block() as blk:
            for rec, reg in ((self.TE, blk.tensor), (self.VE, blk.vector),
                             (self.SE, blk.scalar), (self.SY, blk.sync),
                             (self.GP, blk.gpsimd)):
                if rec.ops:
                    reg(lambda e, r=rec: r.replay(e))

    def build(self):
        nc, T = self.nc, self.T
        NTT, NRT = self.NTT, self.NRT
        S = self.S

        def din(name, shape, dt):
            return nc.dram_tensor(name, shape, dt, kind="ExternalInput").ap()

        def dscr(name, shape, dt, force_out=False):
            kind = "ExternalOutput" if (force_out or self.dbg) else "Internal"
            return nc.dram_tensor(name, shape, dt, kind=kind).ap()

        x = din("x", [T, D], F32)
        qa_w = din("qa_w", [NQAC, 128, NDC * 128], F32)
        qb_w = din("qb_w", [NQBF, 128, NQAC * 128], F32)
        kva_w = din("kva_w", [128, NDC * 576], F32)
        kvb_w = din("kvb_w", [128, NKVC * H * 256], F32)
        o_w = din("o_w", [NDC, 128, NDC * 128], F32)
        r_w = din("r_w", [128, NDC * E], F32)
        gate_w = din("gate_w", [E, NFF, 128, NDC * 128], F32)
        up_w = din("up_w", [E, NFF, 128, NDC * 128], F32)
        down_w = din("down_w", [E, 4, 128, NFF * 512], F32)
        qa_ln = din("qa_ln", [NQAC, 128], F32)
        kva_ln = din("kva_ln", [NKVC, 128], F32)
        c_cos = din("c_cos", [128, T], F32)
        c_sin = din("c_sin", [128, T], F32)
        c_rot = din("c_rot", [128, 128], F32)
        c_dup = din("c_dup", [64, 128], F32)
        c_dmask = din("c_dmask", [128, 4 * 512], BF16)
        c_ident_bf = din("c_ident_bf", [128, 128], BF16)
        c_ident_f32 = din("c_ident_f32", [128, 128], F32)
        c_ones_bf = din("c_ones_bf", [128, 128], BF16)
        c_ltri = din("c_ltri", [128, 128], BF16)
        c_erow = din("c_erow", [128, E], F32)
        c_onesrow = din("c_onesrow", [1, 128], F32)

        d_kn = dscr("d_kn", [H * 128, T], BF16)
        d_qT = dscr("d_qT", [NTT, 128, NQBF * 512], BF16)
        d_y1bf = nc.dram_tensor("d_y1bf", [T, D], BF16, kind="Internal").ap()
        d_gidx = nc.dram_tensor("d_gidx", [128, 48], mybir.dt.uint32,
                                kind="Internal").ap()
        d_expout = nc.dram_tensor("d_expout", [E * 384 + 128, D], BF16,
                                  kind="Internal").ap()
        out = dscr("out", [T, D], F32, force_out=True)

        sb = lambda n, s, dt: self.stack.enter_context(nc.sbuf_tensor(n, s, dt))

        ident_bf = sb("ident_bf", [128, 128], BF16)
        ident_f32 = sb("ident_f32", [128, 128], F32)
        ones_bf = sb("ones_bf", [128, 128], BF16)
        rot_t = sb("rot_t", [128, 128], F32)
        dup_t = sb("dup_t", [64, 128], BF16)
        qa_ln_t = sb("qa_ln_t", [128, NQAC], F32)
        kva_ln_t = sb("kva_ln_t", [128, NKVC], F32)
        P1 = sb("P1", [128, max(16 * T, 28672)], BF16)
        P2 = sb("P2", [128, 24576], BF16)
        stn = ["lnsum", "lnsumsq", "lnmu", "lnvar", "lntmp", "lnrstd",
               "rtmax", "rtneg", "rtsum", "rtm1", "rtm2", "rtden"]
        st = {n: sb("st_" + n, [128, 1], F32) for n in stn}
        eps_rms = sb("eps_rms", [128, 1], F32)
        eps_ln = sb("eps_ln", [128, 1], F32)
        tsrc = sb("tsrc", [128, 2 * (T // 128)], mybir.dt.uint32)
        tw = sb("tw", [128, 2 * (T // 128)], F32)
        mid_stack = contextlib.ExitStack()
        krope = mid_stack.enter_context(nc.sbuf_tensor("krope", [128, T], BF16))
        mneg_sb = mid_stack.enter_context(nc.sbuf_tensor("mneg_sb", [128, H], F32))
        inner_stack = contextlib.ExitStack()
        cosT = inner_stack.enter_context(nc.sbuf_tensor("cosT", [128, T], F32))
        sinT = inner_stack.enter_context(nc.sbuf_tensor("sinT", [128, T], F32))
        kvn = inner_stack.enter_context(nc.sbuf_tensor("kvn", [128, NKVC * T], BF16))
        kr2 = inner_stack.enter_context(nc.sbuf_tensor("kr2", [1, T], BF16))
        kmax2 = inner_stack.enter_context(nc.sbuf_tensor("kmax2", [1, H], F32))

        t_p, v_p, se_p = S("t_p"), S("v_p"), S("se_p")
        s_y1w = S("s_y1w")
        s_y1s = S("s_y1s")
        s_out = S("s_out")


        ph0 = contextlib.ExitStack()
        xrow = ph0.enter_context(self.sbufu("xrow", [128, 2 * D], F32))
        xrow_bf = ph0.enter_context(self.sbufu("xrow_bf", [128, D], BF16))
        g_tmp = ph0.enter_context(self.sbufu("g_tmp", [NQAC, 128], F32))
        g_tmp2 = ph0.enter_context(self.sbufu("g_tmp2", [NKVC, 128], F32))
        b0_tr0 = ph0.enter_context(self.psumu("b0_tr0", [128, 512], BF16))
        b0_tr1 = ph0.enter_context(self.psumu("b0_tr1", [128, 512], BF16))
        if True:

            self.new_phase()
            TE, VE, SE, SY, GP = self.TE, self.VE, self.SE, self.SY, self.GP
            ld = S("b0_ld")
            for dst, src in [(ident_bf, c_ident_bf), (ident_f32, c_ident_f32),
                             (ones_bf, c_ones_bf),
                             (rot_t, c_rot),

                             (cosT, c_cos), (sinT, c_sin),
                             (g_tmp, qa_ln), (g_tmp2, kva_ln)]:
                SY.incop(ld, "dma_start", dst[:], src[:])
            ldg = S("b0_ldg")
            GP.incop(ldg, "dma_start", dup_t[:], c_dup[:])

            VE.op("memset", eps_rms[:], RMS_EPS)
            VE.op("memset", eps_ln[:], LN_EPS)
            TE.wait(ld)
            TE.wait(ldg)
            b_xrow = [Buf(), Buf()]
            b_xbf = [Buf(), Buf()]
            b_tr = [Buf(), Buf()]
            trbank = [b0_tr0, b0_tr1]
            sx = [S("b0x0"), S("b0x1")]

            def emit_b0row(r):
                p = r % 2
                b_xrow[p].p(SY)
                SY.incop(sx[p], "dma_start", xrow[:, p * D:(p + 1) * D],
                         x[r * 128:(r + 1) * 128, :])
                b_xrow[p].set_ready(sx[p])
                b_xrow[p].c(SE)
                b_xbf[0].p(SE)
                SE.incop(se_p, "activation", xrow_bf[:],
                         xrow[:, p * D:(p + 1) * D], AF.Copy)
                b_xrow[p].add_free(se_p)
                b_xbf[0].set_ready(se_p)
                b_xbf[0].c(TE)
                for g in range(4):
                    gp = (r * 4 + g) % 2
                    b_tr[gp].p(TE)
                    for j in range(4):
                        c = g * 4 + j
                        TE.incop(t_p, "transpose",
                                 trbank[gp][:, j * 128:(j + 1) * 128],
                                 xrow_bf[:, c * 128:(c + 1) * 128],
                                 ident_bf[:])
                    b_tr[gp].set_ready(t_p)
                    if g == 3:
                        b_xbf[0].add_free(t_p)
                    b_tr[gp].c(VE)
                    for j in range(4):
                        c = g * 4 + j
                        VE.incop(v_p, "tensor_copy",
                                 P1[:, c * T + r * 128: c * T + (r + 1) * 128],
                                 trbank[gp][:, j * 128:(j + 1) * 128])
                    b_tr[gp].add_free(v_p)

        ph = contextlib.ExitStack()
        kva_bf = ph.enter_context(self.sbufu("kva_bf", [128, NDC * 576], BF16))
        kvc_bf = ph.enter_context(self.sbufu("kvc_bf", [128, 4 * 512], BF16))
        sqb = ph.enter_context(self.sbufu("sqb", [128, 2 * 512], BF16))
        rstd_t = ph.enter_context(self.sbufu("rstd_t", [128, 512], F32))
        rp32 = ph.enter_context(self.sbufu("rp32", [128, 2 * 512], F32))
        rpb = ph.enter_context(self.sbufu("rpb", [64, 512], BF16))
        sq6 = ph.enter_context(self.sbufu("sq6", [64, 512], BF16))
        p_mm0 = ph.enter_context(self.psumu("p_mm0", [128, 512], F32))
        p_mm1 = ph.enter_context(self.psumu("p_mm1", [128, 512], F32))
        p_st = ph.enter_context(self.psumu("p_st", [128, 512], F32))
        p_ro = ph.enter_context(self.psumu("p_ro", [128, 512], F32))
        p_du = ph.enter_context(self.psumu("p_du", [128, 512], F32))
        p_kr = ph.enter_context(self.psumu("p_kr", [1, 512], F32))
        if True:
            wk = S("w_kva")
            GP.incop(wk, "dma_start", kva_bf[:], kva_w[:])
            TE.incop(t_p, "transpose", p_st[:, 0:NQAC], g_tmp[:],
                     ident_f32[0:NQAC, 0:NQAC])
            VE.wait(t_p)
            VE.incop(v_p, "tensor_copy", qa_ln_t[:], p_st[:, 0:NQAC])
            TE.wait(v_p)
            TE.incop(t_p, "transpose", p_st[:, 0:NKVC], g_tmp2[:],
                     ident_f32[0:NKVC, 0:NKVC])
            VE.wait(t_p)
            VE.incop(v_p, "tensor_copy", kva_ln_t[:], p_st[:, 0:NKVC])
            TE.wait(v_p)
            TE.wait(wk)
            pmm = [Buf(), Buf()]
            pbank = [p_mm0, p_mm1]
            b_kvc = Buf()
            b_sq = [Buf(), Buf()]
            b_pst = Buf()
            b_rstd = Buf()
            b_rp = Buf()
            b_pro = Buf()
            b_pdu = Buf()
            b_rpb = Buf()
            b_sq6 = Buf()
            b_pkr = Buf()
            mi = 0
            for tt in range(NTT):
                t0 = tt * 512
                for r_ in range(4 * tt, 4 * tt + 4):
                    emit_b0row(r_)
                ev_x4 = (v_p, v_p.v)
                TE.wait_abs(*ev_x4)
                b_kvc.p(SE)
                for f in range(NKVC):
                    par = mi % 2
                    pmm[par].p(TE)
                    for c in range(NDC):
                        TE.incop(t_p, "matmul", pbank[par][:],
                                 kva_bf[:, c * 576 + f * 128: c * 576 + (f + 1) * 128],
                                 P1[:, c * T + t0: c * T + t0 + 512],
                                 start=(c == 0), stop=(c == NDC - 1))
                    pmm[par].set_ready(t_p)
                    pmm[par].c(SE)
                    SE.incop(se_p, "activation", kvc_bf[:, f * 512:(f + 1) * 512],
                             pbank[par][:], AF.Copy)
                    pmm[par].add_free(se_p)
                    mi += 1
                b_kvc.set_ready(se_p)
                b_kvc.c(VE)
                for f in range(NKVC):
                    sp = f % 2
                    b_sq[sp].p(VE)
                    VE.incop(v_p, "tensor_tensor", sqb[:, sp * 512:(sp + 1) * 512],
                             kvc_bf[:, f * 512:(f + 1) * 512],
                             kvc_bf[:, f * 512:(f + 1) * 512], ALU.mult)
                    b_sq[sp].set_ready(v_p)
                    b_sq[sp].c(TE)
                    if f == 0:
                        b_pst.p(TE)
                    TE.incop(t_p, "matmul", p_st[:], ones_bf[:],
                             sqb[:, sp * 512:(sp + 1) * 512],
                             start=(f == 0), stop=(f == NKVC - 1))
                    b_sq[sp].add_free(t_p)
                b_pst.set_ready(t_p)
                b_pst.c(SE)
                b_rstd.p(SE)
                SE.op("activation", rstd_t[:], p_st[:], AF.Ln,
                      scale=1.0 / KV_LORA, bias=eps_rms[:])
                SE.incop(se_p, "activation", rstd_t[:], rstd_t[:], AF.Exp, scale=-0.5)
                b_pst.add_free(se_p)
                b_rstd.set_ready(se_p)
                b_rstd.c(VE)
                for f in range(NKVC):
                    VE.incop(v_p, "scalar_tensor_tensor",
                             kvn[:, f * T + t0: f * T + t0 + 512],
                             kvc_bf[:, f * 512:(f + 1) * 512],
                             kva_ln_t[:, f:f + 1], rstd_t[:],
                             ALU.mult, ALU.mult)
                b_rstd.add_free(v_p)
                b_kvc.add_free(v_p)

                b_pro.p(TE)
                for c in range(NDC):
                    TE.incop(t_p, "matmul", p_ro[:64, :],
                             kva_bf[:, c * 576 + 512: c * 576 + 576],
                             P1[:, c * T + t0: c * T + t0 + 512],
                             start=(c == 0), stop=(c == NDC - 1))
                b_pro.set_ready(t_p)
                b_pro.c(SE)
                b_rp.p(SE)
                SE.incop(se_p, "activation", rp32[:64, 0:512], p_ro[:64, :], AF.Copy)
                b_pro.add_free(se_p)
                b_rp.set_ready(se_p)
                b_rp.c(TE)
                b_pro.p(TE)
                TE.incop(t_p, "matmul", p_ro[:64, :], rot_t[0:64, 0:64],
                         rp32[:64, 0:512], start=True, stop=True)
                b_pro.set_ready(t_p)
                b_pro.c(VE)
                b_rpb.p(VE)
                VE.op("tensor_tensor", rp32[:64, 0:512], rp32[:64, 0:512],
                      cosT[0:64, t0:t0 + 512], ALU.mult)
                VE.op("tensor_tensor", rp32[:64, 512:1024], p_ro[:64, :],
                      sinT[0:64, t0:t0 + 512], ALU.mult)
                VE.incop(v_p, "tensor_tensor", rpb[:, :],
                         rp32[:64, 0:512], rp32[:64, 512:1024], ALU.add)
                b_pro.add_free(v_p)
                b_rp.add_free(v_p)
                b_rpb.set_ready(v_p)
                b_rpb.c(TE)
                b_pdu.p(TE)
                TE.incop(t_p, "matmul", p_du[:], dup_t[:], rpb[:, :],
                         start=True, stop=True)
                b_rpb.add_free(t_p)
                b_pdu.set_ready(t_p)
                b_pdu.c(SE)
                SE.incop(se_p, "activation", krope[:, t0:t0 + 512], p_du[:],
                         AF.Copy)
                b_pdu.add_free(se_p)
                b_sq6.p(VE)
                VE.incop(v_p, "tensor_tensor", sq6[:, :],
                         rpb[:, :], rpb[:, :], ALU.mult)
                b_rpb.add_free(v_p)
                b_sq6.set_ready(v_p)
                b_sq6.c(TE)
                b_pkr.p(TE)
                TE.incop(t_p, "matmul", p_kr[0:1, :], ones_bf[0:64, 0:1],
                         sq6[:, :], start=True, stop=True, skip_group_check=True)
                b_sq6.add_free(t_p)
                b_pkr.set_ready(t_p)
                b_pkr.c(VE)
                VE.incop(v_p, "tensor_copy", kr2[0:1, t0:t0 + 512], p_kr[0:1, :])
                b_pkr.add_free(v_p)
            SY.wait(ld)
            GP.wait(ldg)
            self.emit_phase()
            ph.close()
            ph0.close()

        ph = contextlib.ExitStack()
        qaw = ph.enter_context(self.sbufu("qaw", [128, 3 * NDC * 128], BF16))
        sqb2 = ph.enter_context(self.sbufu("sqb2", [128, 2 * 512], BF16))
        rstd2 = ph.enter_context(self.sbufu("rstd2", [128, 512], F32))
        p_mm0 = ph.enter_context(self.psumu("p_mm0", [128, 512], F32))
        p_mm1 = ph.enter_context(self.psumu("p_mm1", [128, 512], F32))
        p_s4 = [ph.enter_context(self.psumu(f"p_s4{i}", [128, 512], F32))
                for i in range(4)]
        if True:

            self.new_phase()
            TE, VE, SE, SY, GP = self.TE, self.VE, self.SE, self.SY, self.GP
            pmm = [Buf(), Buf()]
            pbank = [p_mm0, p_mm1]
            b_w = [Buf() for _ in range(3)]
            sw = [S(f"b2aw{i}") for i in range(3)]
            b_sq = [Buf(), Buf()]
            b_pst = [Buf() for _ in range(NTT)]
            b_rstd = Buf()
            pend_s = []
            mi = 0
            for f in range(NQAC):
                wp = f % 3
                b_w[wp].p(GP)
                GP.incop(sw[wp], "dma_start",
                         qaw[:, wp * NDC * 128:(wp + 1) * NDC * 128], qa_w[f])
                b_w[wp].set_ready(sw[wp])
                b_w[wp].c(TE)
                for tt in range(NTT):
                    t0 = tt * 512
                    par = mi % 2
                    pmm[par].p(TE)
                    for c in range(NDC):
                        TE.incop(t_p, "matmul", pbank[par][:],
                                 qaw[:, wp * NDC * 128 + c * 128:
                                     wp * NDC * 128 + (c + 1) * 128],
                                 P1[:, c * T + t0: c * T + t0 + 512],
                                 start=(c == 0), stop=(c == NDC - 1))
                    pmm[par].set_ready(t_p)
                    for fn_ in pend_s:
                        fn_()
                    pend_s.clear()
                    if tt == NTT - 1:
                        b_w[wp].add_free(t_p)
                    pmm[par].c(SE)
                    SE.incop(se_p, "activation",
                             P2[:, f * T + t0: f * T + t0 + 512],
                             pbank[par][:], AF.Copy)
                    pmm[par].add_free(se_p)
                    sp = mi % 2
                    b_sq[sp].p(VE)
                    VE.wait(se_p)
                    VE.incop(v_p, "tensor_tensor",
                             sqb2[:, sp * 512:(sp + 1) * 512],
                             P2[:, f * T + t0: f * T + t0 + 512],
                             P2[:, f * T + t0: f * T + t0 + 512], ALU.mult)
                    b_sq[sp].set_ready(v_p)

                    def st_fn(sp=sp, f=f, tt=tt):
                        b_sq[sp].c(TE)
                        if f == 0:
                            b_pst[tt].p(TE)
                        TE.incop(t_p, "matmul", p_s4[tt][:], ones_bf[:],
                                 sqb2[:, sp * 512:(sp + 1) * 512],
                                 start=(f == 0), stop=(f == NQAC - 1),
                                 skip_group_check=True)
                        b_sq[sp].add_free(t_p)
                        if f == NQAC - 1:
                            b_pst[tt].set_ready(t_p)
                    pend_s.append(st_fn)
                    mi += 1
            for fn_ in pend_s:
                fn_()
            pend_s.clear()
            for tt in range(NTT):
                t0 = tt * 512
                b_pst[tt].c(SE)
                b_rstd.p(SE)
                SE.op("activation", rstd2[:], p_s4[tt][:], AF.Ln,
                      scale=1.0 / Q_LORA, bias=eps_rms[:])
                SE.incop(se_p, "activation", rstd2[:], rstd2[:], AF.Exp,
                         scale=-0.5)
                b_pst[tt].add_free(se_p)
                b_rstd.set_ready(se_p)
                b_rstd.c(VE)
                for f in range(NQAC):
                    VE.op("scalar_tensor_tensor",
                          P2[:, f * T + t0: f * T + t0 + 512],
                          P2[:, f * T + t0: f * T + t0 + 512],
                          qa_ln_t[:, f:f + 1], rstd2[:], ALU.mult, ALU.mult)
                VE.incop(v_p, "engine_nop") if False else None
                b_rstd.add_free(v_p)
            self.emit_phase()
            ph.close()

        skn = [S("b1bk0"), S("b1bk1")]
        ph = contextlib.ExitStack()
        kvb_bf = ph.enter_context(self.sbufu("kvb_bf", [128, NKVC * H * 256], BF16))
        knst = ph.enter_context(self.sbufu("knst", [128, 2 * 512], BF16))
        vst = ph.enter_context(self.sbufu("vst", [128, 2 * 512], BF16))
        sqk = ph.enter_context(self.sbufu("sqk", [128, 2 * 512], BF16))
        tkrow = ph.enter_context(self.sbufu("tkrow", [1, 512], BF16))
        p_mm0 = ph.enter_context(self.psumu("p_mm0", [128, 512], F32))
        p_mm1 = ph.enter_context(self.psumu("p_mm1", [128, 512], F32))
        p_tr0 = ph.enter_context(self.psumu("p_tr0", [128, 512], BF16))
        p_tr1 = ph.enter_context(self.psumu("p_tr1", [128, 512], BF16))
        p_kn0 = ph.enter_context(self.psumu("p_kn0", [1, 512], F32))
        p_kn1 = ph.enter_context(self.psumu("p_kn1", [1, 512], F32))
        if True:

            self.new_phase()
            TE, VE, SE, SY, GP = self.TE, self.VE, self.SE, self.SY, self.GP
            wkb = S("w_kvb")
            for c in range(NKVC):
                GP.incop(wkb, "dma_start",
                         kvb_bf[:, c * H * 256:(c + 1) * H * 256],
                         kvb_w[:, c * H * 256:(c + 1) * H * 256])
            TE.wait(wkb)
            pmm = [Buf(), Buf()]
            pbank = [p_mm0, p_mm1]
            ptr = [Buf(), Buf()]
            ptbank = [p_tr0, p_tr1]
            pknb = [p_kn0, p_kn1]
            b_knst = [Buf(), Buf()]
            b_vst = [Buf(), Buf()]
            b_sqk = [Buf(), Buf()]
            b_pkn = [Buf(), Buf()]
            pend_k = []
            mi = 0
            for tt in range(NTT):
                t0 = tt * 512
                for fc in range(NKVF):
                    par = mi % 2
                    pmm[par].p(TE)
                    for c in range(NKVC):
                        TE.incop(t_p, "matmul", pbank[par][:],
                                 kvb_bf[:, c * H * 256 + fc * 128:
                                        c * H * 256 + (fc + 1) * 128],
                                 kvn[:, c * T + t0: c * T + t0 + 512],
                                 start=(c == 0), stop=(c == NKVC - 1))
                    pmm[par].set_ready(t_p)
                    for fn_ in pend_k:
                        fn_()
                    pend_k.clear()
                    pmm[par].c(SE)
                    sp = mi % 2
                    if fc < H:
                        b_knst[sp].p(SE)
                        SE.incop(se_p, "activation",
                                 knst[:, sp * 512:(sp + 1) * 512], pbank[par][:],
                                 AF.Copy)
                        pmm[par].add_free(se_p)
                        b_knst[sp].set_ready(se_p)
                        b_knst[sp].c(SY)
                        SY.incop(skn[sp], "dma_start",
                                 d_kn[fc * 128:(fc + 1) * 128, t0:t0 + 512],
                                 knst[:, sp * 512:(sp + 1) * 512])
                        b_knst[sp].add_free(skn[sp])
                        b_sqk[sp].p(SE)
                        SE.incop(se_p, "activation",
                                 sqk[:, sp * 512:(sp + 1) * 512], pbank[par][:],
                                 AF.Square)
                        pmm[par].add_free(se_p)
                        b_sqk[sp].set_ready(se_p)

                        def kmax_fn(sp=sp, tt=tt, fc=fc):
                            b_sqk[sp].c(TE)
                            b_pkn[sp].p(TE)
                            TE.incop(t_p, "matmul", pknb[sp][0:1, :],
                                     ones_bf[:, 0:1],
                                     sqk[:, sp * 512:(sp + 1) * 512],
                                     start=True, stop=True,
                                     skip_group_check=True)
                            b_sqk[sp].add_free(t_p)
                            b_pkn[sp].set_ready(t_p)
                            b_pkn[sp].c(VE)
                            VE.incop(v_p, "tensor_scalar", tkrow[0:1, :],
                                     pknb[sp][0:1, :], 0.0,
                                     None if tt == 0 else kmax2[0:1, fc:fc + 1],
                                     ALU.add, op1=ALU.max,
                                     accum_out=kmax2[0:1, fc:fc + 1])
                            b_pkn[sp].add_free(v_p)
                        pend_k.append(kmax_fn)
                    else:
                        h = fc - H
                        b_vst[sp].p(SE)
                        SE.incop(se_p, "activation",
                                 vst[:, sp * 512:(sp + 1) * 512], pbank[par][:],
                                 AF.Copy)
                        pmm[par].add_free(se_p)
                        b_vst[sp].set_ready(se_p)
                        b_vst[sp].c(TE)
                        ptr[sp].p(TE)
                        for j in range(4):
                            TE.incop(t_p, "transpose",
                                     ptbank[sp][:, j * 128:(j + 1) * 128],
                                     vst[:, sp * 512 + j * 128: sp * 512 + (j + 1) * 128],
                                     ident_bf[:])
                        ptr[sp].set_ready(t_p)
                        b_vst[sp].add_free(t_p)
                        ptr[sp].c(VE)
                        for j in range(4):
                            kt = tt * 4 + j
                            VE.incop(v_p, "tensor_copy",
                                     P1[:, (kt * H + h) * 128:(kt * H + h + 1) * 128],
                                     ptbank[sp][:, j * 128:(j + 1) * 128])
                        ptr[sp].add_free(v_p)
                    mi += 1
            for fn_ in pend_k:
                fn_()
            pend_k.clear()
            self.emit_phase()
            ph.close()

        sqt = [S("b2bq0"), S("b2bq1")]
        ph = contextlib.ExitStack()
        qbw = ph.enter_context(self.sbufu("qbw", [128, 2 * NQAC * 128], BF16))
        qst = ph.enter_context(self.sbufu("qst", [128, 2 * 512], BF16))
        sqn = ph.enter_context(self.sbufu("sqn", [128, 2 * 512], BF16))
        qr32 = ph.enter_context(self.sbufu("qr32", [128, 2 * 512], F32))
        qtmp = ph.enter_context(self.sbufu("qtmp", [128, 512], F32))
        tkq = ph.enter_context(self.sbufu("tkq", [1, 512], BF16))
        qmx = ph.enter_context(self.sbufu("qmx", [1, 2 * H], F32))
        krmax = ph.enter_context(self.sbufu("krmax", [1, 1], F32))
        mrow = ph.enter_context(self.sbufu("mrow", [1, 2 * H], F32))
        mrow_bf = ph.enter_context(self.sbufu("mrow_bf", [1, H], BF16))
        p_mm0 = ph.enter_context(self.psumu("p_mm0", [128, 512], F32))
        p_mm1 = ph.enter_context(self.psumu("p_mm1", [128, 512], F32))
        p_ro4 = ph.enter_context(self.psumu("p_ro4", [128, 512], F32))
        p_m0 = ph.enter_context(self.psumu("p_m0", [1, 512], F32))
        p_m1 = ph.enter_context(self.psumu("p_m1", [1, 512], F32))
        p_mb = ph.enter_context(self.psumu("p_mb", [128, H], F32))
        if True:

            self.new_phase()
            TE, VE, SE, SY, GP = self.TE, self.VE, self.SE, self.SY, self.GP
            VE.incop(v_p, "tensor_reduce", krmax[0:1, :], kr2[0:1, :],
                     AX.X, ALU.max)
            pmm = [Buf(), Buf()]
            pbank = [p_mm0, p_mm1]
            pmb_ = [p_m0, p_m1]
            b_w = [Buf(), Buf()]
            sw = [S("b2bw0"), S("b2bw1")]
            b_qst = [Buf(), Buf()]
            b_sqn = [Buf(), Buf()]
            b_pm = [Buf(), Buf()]
            b_qr = [Buf(), Buf()]
            b_qtmp = Buf()
            b_pro = Buf()
            mi = 0
            pmi = 0

            pend_q = []

            def qmax_col(sqn_ap, ones_ap, col, seed, bfree):
                nonlocal pmi
                pm = pmi % 2
                pmi += 1
                b_pm[pm].p(TE)
                TE.incop(t_p, "matmul", pmb_[pm][0:1, :], ones_ap, sqn_ap,
                         start=True, stop=True, skip_group_check=True)
                if bfree is not None:
                    bfree.add_free(t_p)
                b_pm[pm].set_ready(t_p)
                b_pm[pm].c(VE)
                VE.incop(v_p, "tensor_scalar", tkq[0:1, :], pmb_[pm][0:1, :],
                         0.0, None if seed else qmx[0:1, col:col + 1],
                         ALU.add, op1=ALU.max, accum_out=qmx[0:1, col:col + 1])
                b_pm[pm].add_free(v_p)

            for fc in range(NQBF):
                wp = fc % 2
                b_w[wp].p(GP)
                GP.incop(sw[wp], "dma_start",
                         qbw[:, wp * NQAC * 128:(wp + 1) * NQAC * 128],
                         qb_w[fc])
                b_w[wp].set_ready(sw[wp])
                b_w[wp].c(TE)
                for tt in range(NTT):
                    t0 = tt * 512
                    par = mi % 2
                    pmm[par].p(TE)
                    for c in range(NQAC):
                        TE.incop(t_p, "matmul", pbank[par][:],
                                 qbw[:, wp * NQAC * 128 + c * 128:
                                     wp * NQAC * 128 + (c + 1) * 128],
                                 P2[:, c * T + t0: c * T + t0 + 512],
                                 start=(c == 0), stop=(c == NQAC - 1))
                    pmm[par].set_ready(t_p)
                    for fn_ in pend_q:
                        fn_()
                    pend_q.clear()
                    if tt == NTT - 1:
                        b_w[wp].add_free(t_p)
                    sp = mi % 2
                    pmm[par].c(SE)
                    if fc < H:
                        b_qst[sp].p(SE)
                        SE.incop(se_p, "activation",
                                 qst[:, sp * 512:(sp + 1) * 512], pbank[par][:],
                                 AF.Copy, scale=SCALE)
                        b_qst[sp].set_ready(se_p)
                        b_sqn[sp].p(SE)
                        SE.incop(se_p, "activation",
                                 sqn[:, sp * 512:(sp + 1) * 512], pbank[par][:],
                                 AF.Square, scale=SCALE)
                        pmm[par].add_free(se_p)
                        b_sqn[sp].set_ready(se_p)
                        pend_q.append(
                            lambda a=sqn[:, sp * 512:(sp + 1) * 512],
                            c_=fc, s_=(tt == 0), b=b_sqn[sp]:
                            (b.c(TE), qmax_col(a, ones_bf[:, 0:1], c_, s_, b)))
                    else:
                        b_qr[sp].p(SE)
                        SE.incop(se_p, "activation",
                                 qr32[:, sp * 512:(sp + 1) * 512], pbank[par][:],
                                 AF.Copy, scale=SCALE)
                        pmm[par].add_free(se_p)
                        b_qr[sp].set_ready(se_p)
                        b_qr[sp].c(TE)
                        b_pro.p(TE)
                        TE.incop(t_p, "matmul", p_ro4[:], rot_t[:],
                                 qr32[:, sp * 512:(sp + 1) * 512],
                                 start=True, stop=True)
                        b_pro.set_ready(t_p)
                        b_pro.c(VE)
                        b_qtmp.p(VE)
                        VE.op("tensor_tensor", qtmp[:], p_ro4[:],
                              sinT[:, t0:t0 + 512], ALU.mult)
                        VE.op("tensor_tensor", qr32[:, sp * 512:(sp + 1) * 512],
                              qr32[:, sp * 512:(sp + 1) * 512],
                              cosT[:, t0:t0 + 512], ALU.mult)
                        b_qst[sp].p(VE)
                        VE.incop(v_p, "tensor_tensor",
                                 qst[:, sp * 512:(sp + 1) * 512],
                                 qr32[:, sp * 512:(sp + 1) * 512], qtmp[:],
                                 ALU.add)
                        b_pro.add_free(v_p)
                        b_qr[sp].add_free(v_p)
                        b_qtmp.add_free(v_p)
                        b_qst[sp].set_ready(v_p)
                        b_sqn[sp].p(SE)
                        SE.wait(v_p)
                        SE.incop(se_p, "activation",
                                 sqn[:, sp * 512:(sp + 1) * 512],
                                 qst[:, sp * 512:(sp + 1) * 512], AF.Square)
                        b_qst[sp].add_free(se_p)
                        b_sqn[sp].set_ready(se_p)
                        hA = 2 * (fc - 16)

                        def rope_q(sp=sp, hA=hA, seed=(tt == 0)):
                            b_sqn[sp].c(TE)
                            qmax_col(sqn[0:64, sp * 512:(sp + 1) * 512],
                                     ones_bf[0:64, 0:1], H + hA, seed, None)
                            qmax_col(sqn[64:128, sp * 512:(sp + 1) * 512],
                                     ones_bf[64:128, 0:1], H + hA + 1, seed,
                                     b_sqn[sp])
                        pend_q.append(rope_q)
                    b_qst[sp].c(SY)
                    SY.incop(sqt[sp], "dma_start",
                             d_qT[tt, :, fc * 512:(fc + 1) * 512],
                             qst[:, sp * 512:(sp + 1) * 512])
                    b_qst[sp].add_free(sqt[sp])
                    mi += 1
            for fn_ in pend_q:
                fn_()
            pend_q.clear()
            VE.op("tensor_scalar", mrow[0:1, 0:H], kmax2[0:1, :],
                  krmax[0:1, 0:1], None, ALU.add)
            VE.op("tensor_tensor", mrow[0:1, H:2 * H], qmx[0:1, 0:H],
                  qmx[0:1, H:2 * H], ALU.add)
            VE.incop(v_p, "tensor_tensor", mrow[0:1, 0:H], mrow[0:1, 0:H],
                     mrow[0:1, H:2 * H], ALU.mult)
            SE.wait(v_p)
            SE.incop(se_p, "activation", mrow[0:1, H:2 * H], mrow[0:1, 0:H],
                     AF.Sqrt)
            VE.wait(se_p)
            VE.incop(v_p, "tensor_scalar", mrow_bf[0:1, :], mrow[0:1, H:2 * H],
                     -1.0, None, ALU.mult)
            TE.wait(v_p)
            TE.incop(t_p, "matmul", p_mb[:, 0:H], ones_bf[0:1, 0:128],
                     mrow_bf[0:1, :], start=True, stop=True,
                     skip_group_check=True)
            SE.wait(t_p)
            SE.incop(se_p, "activation", mneg_sb[:, :], p_mb[:, 0:H], AF.Copy)
            self.emit_phase()
            ph.close()
        inner_stack.close()

        ph = contextlib.ExitStack()
        kn_sb = ph.enter_context(self.sbufu("kn_sb", [128, 2 * 2048], BF16))
        dmask_t = ph.enter_context(self.sbufu("dmask_t", [128, 4 * 512], BF16))
        ltri_t = ph.enter_context(self.sbufu("ltri_t", [128, 128], BF16))
        erow_t = ph.enter_context(self.sbufu("erow_t", [128, E], F32))
        onesrow_t = ph.enter_context(self.sbufu("onesrow_t", [1, 128], F32))
        rwt = ph.enter_context(self.sbufu("rwt", [128, NDC * E], BF16))
        prb = ph.enter_context(self.sbufu("prb", [128, 4 * E], F32))
        indall = ph.enter_context(self.sbufu("indall", [128, 4 * E], BF16))
        eqall = ph.enter_context(self.sbufu("eqall", [128, 8 * E], BF16))
        eall = ph.enter_context(self.sbufu("eall", [128, 8], F32))
        slot4 = ph.enter_context(self.sbufu("slot4", [128, 4 * E], F32))
        bases4 = ph.enter_context(self.sbufu("bases4", [1, 4 * E], F32))
        tmp16 = ph.enter_context(self.sbufu("tmp16", [128, E], F32))
        base = ph.enter_context(self.sbufu("base", [128, E], F32))
        au8 = ph.enter_context(self.sbufu("au8", [128, 8], mybir.dt.uint32))
        adrf = ph.enter_context(self.sbufu("adrf", [128, 2], F32))
        dcol = ph.enter_context(self.sbufu("dcol", [128, 4], F32))
        wrow = ph.enter_context(self.sbufu("wrow", [128, 4], F32))
        gsent = ph.enter_context(self.sbufu("gsent", [128, E * 3],
                                            mybir.dt.uint32))
        tok32 = ph.enter_context(self.sbufu("tok32", [128, 1],
                                            mybir.dt.uint32))
        eT = ph.enter_context(self.sbufu("eT", [128, 3 * 512], BF16))
        rczf = ph.enter_context(self.sbufu("rczf", [1, 512], F32))
        rcz = ph.enter_context(self.sbufu("rcz", [1, 2 * 512], BF16))
        zbc = ph.enter_context(self.sbufu("zbc", [128, 2 * 512], BF16))
        owt = ph.enter_context(self.sbufu("owt", [128, 3 * 2048], BF16))
        oevac = ph.enter_context(self.sbufu("oevac", [128, 2 * 512], F32))
        atm = ph.enter_context(self.sbufu("atm", [128, 4 * 2048], BF16))
        xrow3 = ph.enter_context(self.sbufu("xrow3", [128, 2 * D], F32))
        y1row = ph.enter_context(self.sbufu("y1row", [128, D], BF16))
        y1tst = ph.enter_context(self.sbufu("y1tst", [128, NDC * 128], BF16))
        p_s0 = ph.enter_context(self.psumu("p_s0", [128, 512], F32))
        p_s1 = ph.enter_context(self.psumu("p_s1", [128, 512], F32))
        p_av0 = ph.enter_context(self.psumu("p_av0", [128, 512], F32))
        p_av1 = ph.enter_context(self.psumu("p_av1", [128, 512], F32))
        p_z0 = ph.enter_context(self.psumu("p_z0", [128, 512], F32))
        p_z1 = ph.enter_context(self.psumu("p_z1", [128, 512], F32))
        p_t0 = ph.enter_context(self.psumu("p_t0", [128, 512], BF16))
        p_t1 = ph.enter_context(self.psumu("p_t1", [128, 512], BF16))
        if True:

            self.new_phase()
            TE, VE, SE, SY, GP = self.TE, self.VE, self.SE, self.SY, self.GP
            QN_OFF, AO_OFF = 0, 12288
            s_dm = S("b3dm")
            GP.incop(s_dm, "dma_start", dmask_t[:], c_dmask[:])
            GP.incop(s_dm, "dma_start", rwt[:], r_w[:])
            for dst_, src_ in [(ltri_t, c_ltri), (erow_t, c_erow),
                               (onesrow_t, c_onesrow)]:
                GP.incop(s_dm, "dma_start", dst_[:], src_[:])
            VE.wait(s_dm)
            s_init = S("b3init")
            breg_s = nc.gpsimd.alloc_register()
            GP.op("reg_mov", breg_s, 128 * E * 3 - 1)
            gp_c = S("gp_c")
            GP.incop(gp_c, "iota", gsent[:], [[0, 1], [0, 1], [0, E], [128, 3]],
                     base=T, channel_multiplier=1)
            GP.wait_force(gp_c)
            GP.incop(s_init, "dma_start", d_gidx[:], gsent[:])
            VE.op("memset", base[0:1, :], 0.0)
            s_scat = S("b3scat")
            TE.wait(s_dm)

            def router_top(rt):
                j = rt % 4
                pv = rt % 2
                b_pav[pv].p(TE)
                for c in range(NDC):
                    TE.incop(t_p, "matmul", pavb[pv][:, 0:E],
                             y1tst[:, c * 128:(c + 1) * 128],
                             rwt[:, c * E:(c + 1) * E],
                             start=(c == 0), stop=(c == NDC - 1),
                             skip_group_check=True)
                b_pav[pv].set_ready(t_p)
                b_y1t.add_free(t_p)
                b_pav[pv].c(VE)
                VE.incop(v_p, "tensor_copy", prb[:, 0:E], pavb[pv][:, 0:E])
                b_pav[pv].add_free(v_p)
                VE.op("tensor_reduce", st["rtm1"][:], prb[:, 0:E], AX.X,
                      ALU.max)
                VE.op("tensor_scalar", prb[:, E:2 * E], prb[:, 0:E],
                      st["rtm1"][:], None, ALU.is_equal)
                VE.op("scalar_tensor_tensor", prb[:, 2 * E:3 * E],
                      prb[:, E:2 * E], -1.0e30, prb[:, 0:E], ALU.mult,
                      ALU.add)
                VE.op("tensor_reduce", st["rtm2"][:], prb[:, 2 * E:3 * E],
                      AX.X, ALU.max)
                VE.op("tensor_scalar", prb[:, 3 * E:4 * E], prb[:, 2 * E:3 * E],
                      st["rtm2"][:], None, ALU.is_equal)
                VE.op("tensor_tensor", dcol[:, j:j + 1], st["rtm1"][:],
                      st["rtm2"][:], ALU.subtract)
                VE.op("tensor_tensor", tmp16[:], prb[:, E:2 * E],
                      prb[:, 3 * E:4 * E], ALU.add)
                VE.op("tensor_copy", indall[:, j * E:(j + 1) * E], tmp16[:])
                for k in range(2):
                    eqs = prb[:, E:2 * E] if k == 0 else prb[:, 3 * E:4 * E]
                    VE.op("tensor_copy",
                          eqall[:, (j * 2 + k) * E:(j * 2 + k + 1) * E], eqs)
                    VE.op("tensor_tensor", tmp16[:], eqs, erow_t[:], ALU.mult)
                    VE.incop(v_p, "tensor_reduce",
                             eall[:, j * 2 + k: j * 2 + k + 1],
                             tmp16[:], AX.X, ALU.add)

            def router_finish(rts):
                CAP = 384
                js = [r_ % 4 for r_ in rts]
                lo, hi = js[0] * E, (js[-1] + 1) * E
                SE.wait(v_p)
                SE.incop(se_p, "activation", wrow[:, js[0]:js[-1] + 1],
                         dcol[:, js[0]:js[-1] + 1], AF.Sigmoid)
                VE.wait(se_p)
                for ji, rt in zip(js, rts):
                    VE.op("tensor_copy", tw[:, rt * 2: rt * 2 + 1],
                          wrow[:, ji:ji + 1])
                    VE.incop(v_p, "tensor_scalar",
                             tw[:, rt * 2 + 1: rt * 2 + 2],
                             wrow[:, ji:ji + 1], -1.0, 1.0, ALU.mult,
                             op1=ALU.add)
                b_pav[0].p(TE)
                TE.wait(v_p)
                TE.incop(t_p, "matmul", pavb[0][:, lo:hi], ltri_t[:],
                         indall[:, lo:hi], start=True, stop=True,
                         skip_group_check=True)
                b_pav[0].set_ready(t_p)
                b_pav[1].p(TE)
                TE.incop(t_p, "matmul", pavb[1][0:1, lo:hi], ones_bf[:, 0:1],
                         indall[:, lo:hi], start=True, stop=True,
                         skip_group_check=True)
                b_pav[1].set_ready(t_p)
                b_pav[1].c(VE)
                for ji in js:
                    VE.op("tensor_copy", bases4[0:1, ji * E:(ji + 1) * E],
                          base[0:1, 0:E])
                    VE.incop(v_p, "tensor_tensor", base[0:1, 0:E],
                             base[0:1, 0:E],
                             pavb[1][0:1, ji * E:(ji + 1) * E], ALU.add)
                b_pav[1].add_free(v_p)
                b_pav[1].p(TE)
                TE.wait(v_p)
                TE.incop(t_p, "matmul", pavb[1][:, lo:hi], onesrow_t[:],
                         bases4[0:1, lo:hi], start=True, stop=True,
                         skip_group_check=True)
                b_pav[1].set_ready(t_p)
                b_pav[0].c(VE)
                VE.op("tensor_scalar", slot4[:, lo:hi], pavb[0][:, lo:hi],
                      -1.0, None, ALU.add)
                b_pav[0].add_free(v_p)
                b_pav[1].c(VE)
                VE.incop(v_p, "tensor_tensor", slot4[:, lo:hi],
                         slot4[:, lo:hi], pavb[1][:, lo:hi], ALU.add)
                b_pav[1].add_free(v_p)
                VE.wait(s_scat)
                for ji, rt in zip(js, rts):
                    for k in range(2):
                        ce = ji * 2 + k
                        eq_ap = eqall[:, ce * E:(ce + 1) * E]
                        VE.op("tensor_tensor", tmp16[:], eq_ap,
                              slot4[:, ji * E:(ji + 1) * E], ALU.mult)
                        VE.op("tensor_reduce", st["lnsum"][:], tmp16[:],
                              AX.X, ALU.add)
                        VE.op("tensor_scalar", st["lnvar"][:], st["lnsum"][:],
                              128.0, None, ALU.is_ge)
                        VE.op("tensor_scalar", st["lntmp"][:], st["lnsum"][:],
                              256.0, None, ALU.is_ge)
                        VE.op("tensor_tensor", st["lnvar"][:], st["lnvar"][:],
                              st["lntmp"][:], ALU.add)
                        VE.op("scalar_tensor_tensor", st["lnsumsq"][:],
                              st["lnvar"][:], -128.0, st["lnsum"][:], ALU.mult,
                              ALU.add)
                        VE.op("tensor_scalar", adrf[:, 0:1], st["lnsumsq"][:],
                              float(E * 3), None, ALU.mult)
                        VE.op("scalar_tensor_tensor", adrf[:, 0:1],
                              eall[:, ce:ce + 1], 3.0, adrf[:, 0:1], ALU.mult,
                              ALU.add)
                        VE.op("tensor_tensor", adrf[:, 0:1], adrf[:, 0:1],
                              st["lnvar"][:], ALU.add)
                        VE.op("tensor_copy", au8[:, ce:ce + 1], adrf[:, 0:1])
                        VE.op("scalar_tensor_tensor", st["lntmp"][:],
                              eall[:, ce:ce + 1], float(CAP), st["lnsum"][:],
                              ALU.mult, ALU.add)
                        VE.op("tensor_scalar", st["lnvar"][:], st["lnsum"][:],
                              float(CAP), None, ALU.is_ge)
                        VE.op("tensor_scalar", st["lnsumsq"][:], st["lntmp"][:],
                              -1.0, float(E * CAP), ALU.mult, op1=ALU.add)
                        VE.op("tensor_tensor", st["lnsumsq"][:],
                              st["lnsumsq"][:], st["lnvar"][:], ALU.mult)
                        VE.op("tensor_tensor", st["lntmp"][:], st["lntmp"][:],
                              st["lnsumsq"][:], ALU.add)
                        VE.incop(v_p, "tensor_copy",
                                 tsrc[:, rt * 2 + k: rt * 2 + k + 1],
                                 st["lntmp"][:])
                for ji, rt in zip(js, rts):
                    GP.wait_force(s_scat)
                    GP.incop(gp_c, "iota", tok32[:], [[0, 1]], base=rt * 128,
                             channel_multiplier=1)
                    GP.wait_force(gp_c)
                    GP.wait(s_init)
                    GP.wait(v_p)
                    for k in range(2):
                        GP.incop(s_scat, "indirect_dma_start",
                                 d_gidx.rearrange("p (c o) -> (p c) o", o=1),
                                 bass.IndirectOffsetOnAxis(
                                     ap=au8[:, ji * 2 + k: ji * 2 + k + 1],
                                     axis=0),
                                 tok32[:, 0:1], None,
                                 bounds_check=breg_s, oob_is_err=False)

            s_qn = S("b3qn")
            b_qn = Buf()
            s_kn = [S("b3kn0"), S("b3kn1")]
            b_kn = [Buf(), Buf()]
            psb = [p_s0, p_s1]
            b_ps = [Buf(), Buf()]
            pavb = [p_av0, p_av1]
            b_pav = [Buf(), Buf()]
            pzb = [p_z0, p_z1]
            b_pz = [Buf(), Buf()]
            ptb = [p_t0, p_t1]
            b_pt = [Buf(), Buf()]
            b_eT = [Buf(), Buf(), Buf()]
            b_rcz = [Buf(), Buf()]
            b_zbc = [Buf(), Buf()]
            b_aofm = Buf()
            s_ow = [S(f"b3ow{i}") for i in range(3)]
            b_ow = [Buf() for _ in range(3)]
            b_oev = [Buf(), Buf()]
            b_atm = [Buf() for _ in range(4)]
            b_xr = [Buf(), Buf()]
            s_xr = [S("b3xr0"), S("b3xr1")]
            b_y1r = Buf()
            b_y1t = Buf()
            s_ybs = [S("ybs0"), S("ybs1"), S("ybs2"), S("ybs3")]
            pend = []
            fin_pend = []

            def flush_pend():
                for (s4, ev, rt) in pend:
                    b_y1t.p(VE)
                    for g4 in range(4):
                        tp = g4 % 2
                        b_pt[tp].p(TE)
                        TE.wait_abs(*ev)
                        for jj in range(4):
                            c = g4 * 4 + jj
                            TE.incop(t_p, "transpose",
                                     ptb[tp][:, jj * 128:(jj + 1) * 128],
                                     atm[:, s4 * 2048 + c * 128:
                                         s4 * 2048 + (c + 1) * 128],
                                     ident_bf[:])
                        b_pt[tp].set_ready(t_p)
                        if g4 == 3:
                            b_atm[s4].add_free(t_p)
                        b_pt[tp].c(VE)
                        VE.incop(v_p, "tensor_copy",
                                 y1tst[:, g4 * 512:(g4 + 1) * 512], ptb[tp][:, :])
                        b_pt[tp].add_free(v_p)
                    b_y1t.set_ready(v_p)
                    b_y1t.c(TE)
                    router_top(rt)
                    fin_pend.append(rt)
                pend.clear()

            def issue_ow(c):
                sl = c % 3
                b_ow[sl].p(GP)
                GP.incop(s_ow[sl], "dma_start",
                         owt[:, sl * 2048:(sl + 1) * 2048], o_w[c])
                b_ow[sl].set_ready(s_ow[sl])

            ln_pend = []

            def emit_ln1(qb_, s_):
                rt = qb_ * 4 + s_
                xp = rt % 2
                xr = xrow3[:, xp * D:(xp + 1) * D]
                b_xr[xp].p(SY)
                SY.incop(s_xr[xp], "dma_start", xr,
                         x[rt * 128:(rt + 1) * 128, :])
                b_xr[xp].set_ready(s_xr[xp])
                b_xr[xp].c(VE)
                VE.op("tensor_tensor", xr,
                      atm[:, s_ * 2048:(s_ + 1) * 2048], xr, ALU.add)
                b_y1r.p(VE)
                VE.op("tensor_scalar", y1row[:], xr, 0.0, None,
                      ALU.add, op1=ALU.add, accum_out=st["lnsum"][:])
                VE.op("tensor_tensor", y1row[:], xr, xr, ALU.mult)
                VE.op("tensor_scalar", y1row[:], y1row[:], 0.0, None,
                      ALU.add, op1=ALU.add, accum_out=st["lnsumsq"][:])
                VE.op("tensor_scalar", st["lnmu"][:], st["lnsum"][:],
                      1.0 / D, None, ALU.mult)
                VE.op("tensor_scalar", st["lnvar"][:], st["lnsumsq"][:],
                      1.0 / D, None, ALU.mult)
                VE.op("tensor_tensor", st["lntmp"][:], st["lnmu"][:],
                      st["lnmu"][:], ALU.mult)
                VE.incop(v_p, "tensor_tensor", st["lnvar"][:],
                         st["lnvar"][:], st["lntmp"][:], ALU.subtract)
                b_y1r.set_ready(v_p)
                b_y1r.add_free(v_p)
                SE.wait(v_p)
                SE.op("activation", st["lnrstd"][:], st["lnvar"][:], AF.Ln,
                      bias=eps_ln[:])
                SE.incop(se_p, "activation", st["lnrstd"][:],
                         st["lnrstd"][:], AF.Exp, scale=-0.5)
                VE.wait(se_p)
                VE.incop(v_p, "tensor_scalar",
                         atm[:, s_ * 2048:(s_ + 1) * 2048], xr,
                         st["lnmu"][:], st["lnrstd"][:], ALU.subtract,
                         op1=ALU.mult)
                b_xr[xp].add_free(v_p)
                SY.wait_abs(v_p, v_p.v)
                SY.incop(s_ybs[s_], "dma_start",
                         d_y1bf[rt * 128:(rt + 1) * 128, :],
                         atm[:, s_ * 2048:(s_ + 1) * 2048])
                b_atm[s_].add_free(s_ybs[s_])
                pend.append((s_, (v_p, v_p.v), rt))

            NQB = T // 512

            def issue_qT(qb_):
                b_qn.p(SE)
                SE.incop(s_qn, "dma_start", P2[:, QN_OFF:QN_OFF + NQBF * 512],
                         d_qT[qb_])
                b_qn.set_ready(s_qn)

            def issue_kn(qb_, h, rec):
                L_ = (qb_ + 1) * 512
                kp_ = h % 2
                b_kn[kp_].p(rec)
                rec.incop(s_kn[kp_], "dma_start",
                          kn_sb[:, kp_ * 2048: kp_ * 2048 + L_],
                          d_kn[h * 128:(h + 1) * 128, 0:L_])
                b_kn[kp_].set_ready(s_kn[kp_])

            issue_qT(0)
            issue_kn(0, 0, SY)
            issue_kn(0, 1, SY)
            for qb in range(NQB):
                L = (qb + 1) * 512
                nkt = 4 * (qb + 1)
                for c4 in range(3):
                    issue_ow(c4)
                for h in range(H):
                    if ln_pend and h < 4:
                        emit_ln1(*ln_pend.pop(0))
                    kp = h % 2
                    hp = h % 2
                    b_pav[hp].p(TE)
                    b_pz[hp].p(TE)
                    if h == 0:
                        b_aofm.p(VE)
                    avq = []

                    def emit_av(kt):
                        es = kt % 3
                        b_eT[es].c(TE)
                        TE.incop(t_p, "matmul", pavb[hp][:],
                                 P1[:, (kt * H + h) * 128:(kt * H + h + 1) * 128],
                                 eT[:, es * 512:(es + 1) * 512],
                                 start=(kt == 0), stop=(kt == nkt - 1),
                                 skip_group_check=True)
                        TE.incop(t_p, "matmul", pzb[hp][0:1, :],
                                 ones_bf[:, 0:1],
                                 eT[:, es * 512:(es + 1) * 512],
                                 start=(kt == 0), stop=(kt == nkt - 1),
                                 skip_group_check=True)
                        b_eT[es].add_free(t_p)

                    for kt in range(nkt):
                        sp = kt % 2
                        es = kt % 3
                        b_ps[sp].p(TE)
                        b_qn.c(TE)
                        b_kn[kp].c(TE)
                        TE.incop(t_p, "matmul", psb[sp][:],
                                 kn_sb[:, kp * 2048 + kt * 128:
                                       kp * 2048 + (kt + 1) * 128],
                                 P2[:, QN_OFF + h * 512: QN_OFF + (h + 1) * 512],
                                 start=True, stop=False, skip_group_check=True)
                        hb = (h % 2) * 64
                        TE.incop(t_p, "matmul", psb[sp][:],
                                 krope[hb:hb + 64, kt * 128:(kt + 1) * 128],
                                 P2[hb:hb + 64, (16 + h // 2) * 512:
                                    (16 + h // 2) * 512 + 512],
                                 start=False, stop=True, skip_group_check=True)
                        b_ps[sp].set_ready(t_p)
                        if kt == nkt - 1:
                            b_kn[kp].add_free(t_p)
                            if h == H - 1:
                                b_qn.add_free(t_p)
                        b_ps[sp].c(SE)
                        b_eT[es].p(SE)
                        SE.incop(se_p, "activation",
                                 eT[:, es * 512:(es + 1) * 512],
                                 psb[sp][:], AF.Exp, bias=mneg_sb[:, h:h + 1])
                        b_ps[sp].add_free(se_p)
                        if kt >= nkt - 4:
                            jloc = kt - (nkt - 4)
                            VE.wait(se_p)
                            b_eT[es].p(VE)
                            VE.incop(v_p, "tensor_tensor",
                                     eT[:, es * 512:(es + 1) * 512],
                                     eT[:, es * 512:(es + 1) * 512],
                                     dmask_t[:, jloc * 512:(jloc + 1) * 512],
                                     ALU.mult)
                            b_eT[es].set_ready(v_p)
                        else:
                            b_eT[es].set_ready(se_p)
                        avq.append(kt)
                        if len(avq) > 2:
                            emit_av(avq.pop(0))
                    if h + 2 < H:
                        issue_kn(qb, h + 2, SY)
                    while avq:
                        emit_av(avq.pop(0))
                    b_pav[hp].set_ready(t_p)
                    b_pz[hp].set_ready(t_p)
                    b_pz[hp].c(SE)
                    b_rcz[hp].p(SE)
                    SE.op("activation", rczf[0:1, :], pzb[hp][0:1, :], AF.Ln)
                    SE.incop(se_p, "activation",
                             rcz[0:1, hp * 512:(hp + 1) * 512],
                             rczf[0:1, :], AF.Exp, scale=-1.0)
                    b_rcz[hp].set_ready(se_p)
                    b_rcz[hp].c(TE)
                    TE.wait(se_p)
                    TE.incop(t_p, "matmul", pzb[hp][:], ones_bf[0:1, 0:128],
                             rcz[0:1, hp * 512:(hp + 1) * 512],
                             start=True, stop=True, skip_group_check=True)
                    b_rcz[hp].add_free(t_p)
                    b_pz[hp].set_ready(t_p)
                    b_pz[hp].c(SE)
                    b_zbc[hp].p(SE)
                    SE.incop(se_p, "activation",
                             zbc[:, hp * 512:(hp + 1) * 512],
                             pzb[hp][:], AF.Copy)
                    b_pz[hp].add_free(se_p)
                    b_zbc[hp].set_ready(se_p)
                    b_pav[hp].c(VE)
                    b_zbc[hp].c(VE)
                    VE.incop(v_p, "tensor_tensor",
                             P2[:, AO_OFF + h * 512: AO_OFF + (h + 1) * 512],
                             pavb[hp][:], zbc[:, hp * 512:(hp + 1) * 512],
                             ALU.mult)
                    b_pav[hp].add_free(v_p)
                    b_zbc[hp].add_free(v_p)
                    if h == H - 1:
                        b_aofm.set_ready(v_p)
                if qb + 1 < NQB:
                    issue_qT(qb + 1)
                    issue_kn(qb + 1, 0, SE)
                    issue_kn(qb + 1, 1, SE)
                flush_pend()
                b_aofm.c(TE)
                for s4x in range(4):
                    b_atm[s4x].p(VE)
                    b_atm[s4x].free = []
                for c in range(NDC):
                    sl = c % 3
                    op_ = c % 2
                    b_ow[sl].c(TE)
                    b_ps[op_].p(TE)
                    for hv in range(NDC):
                        TE.incop(t_p, "matmul", psb[op_][:],
                                 owt[:, sl * 2048 + hv * 128:
                                     sl * 2048 + (hv + 1) * 128],
                                 P2[:, AO_OFF + hv * 512: AO_OFF + (hv + 1) * 512],
                                 start=(hv == 0), stop=(hv == NDC - 1))
                    b_ps[op_].set_ready(t_p)
                    b_ow[sl].add_free(t_p)
                    if c + 3 < NDC:
                        issue_ow(c + 3)
                    if c == NDC - 1:
                        b_aofm.add_free(t_p)
                    b_ps[op_].c(SE)
                    b_oev[op_].p(SE)
                    SE.incop(se_p, "activation",
                             oevac[:, op_ * 512:(op_ + 1) * 512], psb[op_][:],
                             AF.Copy)
                    b_ps[op_].add_free(se_p)
                    b_oev[op_].set_ready(se_p)
                    b_oev[op_].c(TE)
                    b_pz[op_].p(TE)
                    for s_ in range(4):
                        TE.incop(t_p, "transpose",
                                 pzb[op_][:, s_ * 128:(s_ + 1) * 128],
                                 oevac[:, op_ * 512 + s_ * 128:
                                       op_ * 512 + (s_ + 1) * 128],
                                 ident_f32[:])
                    b_pz[op_].set_ready(t_p)
                    b_oev[op_].add_free(t_p)
                    b_pz[op_].c(VE)
                    for s_ in range(4):
                        VE.incop(v_p, "tensor_copy",
                                 atm[:, s_ * 2048 + c * 128:
                                     s_ * 2048 + (c + 1) * 128],
                                 pzb[op_][:, s_ * 128:(s_ + 1) * 128])
                    b_pz[op_].add_free(v_p)
                if fin_pend:
                    router_finish(fin_pend)
                    fin_pend = []
                ln_pend.extend((qb, s_) for s_ in range(4))
            while ln_pend:
                emit_ln1(*ln_pend.pop(0))
            flush_pend()
            if fin_pend:
                router_finish(fin_pend)
                fin_pend = []
            self.emit_phase()
            ph.close()
        mid_stack.close()
        CAP = 384
        NBLK = E * 3
        ph = contextlib.ExitStack()
        gidx_sb = ph.enter_context(self.sbufu("gidx_sb", [128, NBLK], mybir.dt.uint32))
        gwt = ph.enter_context(self.sbufu("gwt", [128, 3 * NDC * 128], BF16))
        uwt = ph.enter_context(self.sbufu("uwt", [128, 3 * NDC * 128], BF16))
        dwq = ph.enter_context(self.sbufu("dwq", [128, 4 * NFF * 512], BF16))
        sgb = ph.enter_context(self.sbufu("sgb", [128, 2 * 512], BF16))
        uscb = ph.enter_context(self.sbufu("uscb", [128, 2 * 512], BF16))
        stA = ph.enter_context(self.sbufu("stA", [128, 2 * D], BF16))
        stage = stA
        stB = ph.enter_context(self.sbufu("stB", [128, 2 * D], BF16))
        accrow = ph.enter_context(self.sbufu("accrow", [128, D], F32))
        y1r4 = ph.enter_context(self.sbufu("y1r4", [128, D], BF16))
        y2row = ph.enter_context(self.sbufu("y2row", [128, D], F32))
        p0 = ph.enter_context(self.psumu("p0", [128, 512], F32))
        p1 = ph.enter_context(self.psumu("p1", [128, 512], F32))
        p2q = ph.enter_context(self.psumu("p2q", [128, 512], F32))
        p3 = ph.enter_context(self.psumu("p3", [128, 512], F32))
        p4 = ph.enter_context(self.psumu("p4", [128, 512], F32))
        p5 = ph.enter_context(self.psumu("p5", [128, 512], F32))
        p6 = ph.enter_context(self.psumu("p6", [128, 512], BF16))
        p7 = ph.enter_context(self.psumu("p7", [128, 512], BF16))
        if True:
            self.new_phase()
            TE, VE, SE, SY, GP = self.TE, self.VE, self.SE, self.SY, self.GP
            XG_OFF = 0
            XGT_OFF = 8 * 2048
            b_p = [Buf() for _ in range(8)]
            pb = [p0, p1, p2q, p3, p4, p5, p6, p7]

            s_ez = S("b4ez")
            VE.incop(v_p, "memset", stage[:, 0:D], 0.0)
            SE.wait(v_p)
            SE.incop(s_ez, "dma_start", d_expout[E * CAP:E * CAP + 128, :],
                     stage[:, 0:D])
            breg_g = nc.gpsimd.alloc_register()
            breg_e = nc.gpsimd.alloc_register()
            GP.op("reg_mov", breg_g, T - 1)
            GP.op("reg_mov", breg_e, E * CAP + 127)

            s_gl = S("b4gl")
            SY.wait(s_scat)
            SY.incop(s_gl, "dma_start", gidx_sb[:], d_gidx[:])

            b_xg = [Buf() for _ in range(8)]
            s_xg = [S(f"b4xg{i}") for i in range(8)]
            b_xgT = [Buf(), Buf()]
            b_gw = [Buf() for _ in range(3)]
            b_uw = [Buf() for _ in range(3)]
            s_gw = [S(f"b4g{i}") for i in range(3)]
            s_uw = [S(f"b4u{i}") for i in range(3)]
            b_dw = [Buf() for _ in range(4)]
            s_dw = [S(f"b4d{i}") for i in range(4)]
            b_stg = [Buf(), Buf()]
            s_st = [S("b4s0"), S("b4s1")]
            b_stg[0].add_free(s_ez)
            b_h = [Buf(), Buf()]
            b_sgb = [Buf(), Buf()]

            def issue_gu(idx):
                if idx >= E * NFF:
                    return
                e_, f_ = idx // NFF, idx % NFF
                wp_ = idx % 3
                b_gw[wp_].p(GP)
                GP.incop(s_gw[wp_], "dma_start",
                         gwt[:, wp_ * NDC * 128:(wp_ + 1) * NDC * 128],
                         gate_w[e_, f_])
                b_gw[wp_].set_ready(s_gw[wp_])
                b_uw[wp_].p(GP)
                GP.incop(s_uw[wp_], "dma_start",
                         uwt[:, wp_ * NDC * 128:(wp_ + 1) * NDC * 128],
                         up_w[e_, f_])
                b_uw[wp_].set_ready(s_uw[wp_])

            def issue_xg(idx):
                if idx >= E * 3:
                    return
                e_, b_ = idx // 3, idx % 3
                sl = idx % 8
                b_xg[sl].p(GP)
                GP.wait(s_gl)
                GP.incop(s_xg[sl], "indirect_dma_start",
                         P1[:, XG_OFF + sl * 2048: XG_OFF + (sl + 1) * 2048],
                         None, d_y1bf[:],
                         bass.IndirectOffsetOnAxis(
                             ap=gidx_sb[:, e_ * 3 + b_: e_ * 3 + b_ + 1],
                             axis=0),
                         bounds_check=breg_g, oob_is_err=False)
                b_xg[sl].set_ready(s_xg[sl])

            for idx0 in range(3):
                issue_gu(idx0)
            for idx0 in range(6):
                issue_xg(idx0)
            dwi = 0
            gi = 0
            for e in range(E):
                epar = e % 2
                dsl = {}
                for ds in range(4):
                    sl = dwi % 4
                    dwi += 1
                    dsl[ds] = sl
                    b_dw[sl].p(GP)
                    GP.incop(s_dw[sl], "dma_start",
                             dwq[:, sl * NFF * 512:(sl + 1) * NFF * 512],
                             down_w[e, ds])
                    b_dw[sl].set_ready(s_dw[sl])
                b_xgT[epar].p(TE)
                for b_ in range(3):
                    sl = (e * 3 + b_) % 8
                    b_xg[sl].c(TE)
                    for c in range(NDC):
                        tp_ = c % 2
                        b_p[6 + tp_].p(TE)
                        TE.incop(t_p, "transpose",
                                 pb[6 + tp_][:, 0:128],
                                 P1[:, XG_OFF + sl * 2048 + c * 128:
                                    XG_OFF + sl * 2048 + (c + 1) * 128],
                                 ident_bf[:])
                        b_p[6 + tp_].set_ready(t_p)
                        if c == NDC - 1:
                            b_xg[sl].add_free(t_p)
                        b_p[6 + tp_].c(SE)
                        SE.incop(se_p, "activation",
                                 P1[:, XGT_OFF + epar * 6144 + c * 384 + b_ * 128:
                                    XGT_OFF + epar * 6144 + c * 384 + (b_ + 1) * 128],
                                 pb[6 + tp_][:, 0:128], AF.Copy)
                        b_p[6 + tp_].add_free(se_p)
                for b_ in range(3):
                    issue_xg((e + 2) * 3 + b_)
                xgt_ev = (se_p, se_p.v)
                b_xgT[epar].set_ready(se_p)
                b_h[epar].p(VE)
                for f in range(NFF):
                    wp = gi % 3
                    gp_ = f % 2
                    b_gw[wp].c(TE)
                    b_uw[wp].c(TE)
                    TE.wait_abs(*xgt_ev)
                    b_p[gp_].p(TE)
                    for c in range(NDC):
                        TE.incop(t_p, "matmul", pb[gp_][:, 0:CAP],
                                 gwt[:, wp * NDC * 128 + c * 128:
                                     wp * NDC * 128 + (c + 1) * 128],
                                 P1[:, XGT_OFF + epar * 6144 + c * 384:
                                    XGT_OFF + epar * 6144 + (c + 1) * 384],
                                 start=(c == 0), stop=(c == NDC - 1))
                    b_p[gp_].set_ready(t_p)
                    b_gw[wp].add_free(t_p)
                    b_p[2 + gp_].p(TE)
                    for c in range(NDC):
                        TE.incop(t_p, "matmul", pb[2 + gp_][:, 0:CAP],
                                 uwt[:, wp * NDC * 128 + c * 128:
                                     wp * NDC * 128 + (c + 1) * 128],
                                 P1[:, XGT_OFF + epar * 6144 + c * 384:
                                    XGT_OFF + epar * 6144 + (c + 1) * 384],
                                 start=(c == 0), stop=(c == NDC - 1))
                    b_p[2 + gp_].set_ready(t_p)
                    b_uw[wp].add_free(t_p)
                    issue_gu(gi + 3)
                    gi += 1
                    if f == NFF - 1:
                        b_xgT[epar].add_free(t_p)
                    b_p[gp_].c(SE)
                    b_sgb[gp_].p(SE)
                    SE.incop(se_p, "activation",
                             sgb[:, gp_ * 512: gp_ * 512 + CAP],
                             pb[gp_][:, 0:CAP], AF.Silu)
                    b_p[gp_].add_free(se_p)
                    sgb_ev = (se_p, se_p.v)
                    b_p[2 + gp_].c(VE)
                    VE.incop(v_p, "tensor_copy",
                             uscb[:, gp_ * 512: gp_ * 512 + CAP],
                             pb[2 + gp_][:, 0:CAP])
                    b_p[2 + gp_].add_free(v_p)
                    VE.wait_abs(*sgb_ev)
                    VE.incop(v_p, "tensor_tensor",
                             P2[:, (epar * NFF + f) * CAP:
                                (epar * NFF + f + 1) * CAP],
                             sgb[:, gp_ * 512: gp_ * 512 + CAP],
                             uscb[:, gp_ * 512: gp_ * 512 + CAP], ALU.mult)
                    b_sgb[gp_].add_free(v_p)
                b_h[epar].set_ready(v_p)
                b_h[epar].c(TE)
                for m in range(3):
                    spar = m % 2
                    for ds in range(4):
                        dpar = 4 + (ds % 2)
                        sl = dsl[ds]
                        b_dw[sl].c(TE)
                        b_p[dpar].p(TE)
                        for f in range(NFF):
                            TE.incop(t_p, "matmul", pb[dpar][:],
                                     P2[:, (epar * NFF + f) * CAP + m * 128:
                                        (epar * NFF + f) * CAP + (m + 1) * 128],
                                     dwq[:, sl * NFF * 512 + f * 512:
                                         sl * NFF * 512 + (f + 1) * 512],
                                     start=(f == 0), stop=(f == NFF - 1))
                        b_p[dpar].set_ready(t_p)
                        if m == 2:
                            b_dw[sl].add_free(t_p)
                        b_p[dpar].c(VE)
                        if ds == 0:
                            b_stg[spar].p(VE)
                        VE.incop(v_p, "tensor_copy",
                                 stage[:, spar * D + ds * 512:
                                       spar * D + (ds + 1) * 512], pb[dpar][:])
                        b_p[dpar].add_free(v_p)
                    b_stg[spar].set_ready(v_p)
                    b_stg[spar].c(SY)
                    SY.incop(s_st[spar], "dma_start",
                             d_expout[e * CAP + m * 128: e * CAP + (m + 1) * 128, :],
                             stage[:, spar * D:(spar + 1) * D])
                    b_stg[spar].add_free(s_st[spar])
                b_h[epar].add_free(t_p)

            GP.wait(s_st[0])
            GP.wait(s_st[1])
            GP.wait(s_ez)
            b_gA = [Buf(), Buf()]
            b_gB = [Buf(), Buf()]
            s_gA = [S("b4ga0"), S("b4ga1")]
            s_gB = [S("b4gb0"), S("b4gb1")]
            b_y14 = [Buf(), Buf()]
            s_y14 = [S("b4y140"), S("b4y141")]
            b_acr = Buf()
            b_y2 = Buf()
            for rt in range(NRT):
                cp = rt % 2
                b_gA[cp].p(GP)
                GP.incop(s_gA[cp], "indirect_dma_start",
                         stA[:, cp * D:(cp + 1) * D], None, d_expout[:],
                         bass.IndirectOffsetOnAxis(
                             ap=tsrc[:, rt * 2: rt * 2 + 1], axis=0),
                         bounds_check=breg_e, oob_is_err=False)
                b_gA[cp].set_ready(s_gA[cp])
                b_gB[cp].p(GP)
                GP.incop(s_gB[cp], "indirect_dma_start",
                         stB[:, cp * D:(cp + 1) * D], None, d_expout[:],
                         bass.IndirectOffsetOnAxis(
                             ap=tsrc[:, rt * 2 + 1: rt * 2 + 2], axis=0),
                         bounds_check=breg_e, oob_is_err=False)
                b_gB[cp].set_ready(s_gB[cp])
                b_y14[0].p(SY)
                SY.incop(s_y14[0], "dma_start", y1r4[:],
                         d_y1bf[rt * 128:(rt + 1) * 128, :])
                b_y14[0].set_ready(s_y14[0])
                b_gA[cp].c(VE)
                b_y14[0].c(VE)
                b_acr.p(VE)
                VE.op("scalar_tensor_tensor", accrow[:],
                      stA[:, cp * D:(cp + 1) * D], tw[:, rt * 2: rt * 2 + 1],
                      y1r4[:], ALU.mult, ALU.add)
                b_y14[0].add_free(v_p)
                b_gB[cp].c(VE)
                VE.incop(v_p, "scalar_tensor_tensor", accrow[:],
                         stB[:, cp * D:(cp + 1) * D],
                         tw[:, rt * 2 + 1: rt * 2 + 2], accrow[:],
                         ALU.mult, ALU.add)
                b_gA[cp].add_free(v_p)
                b_gB[cp].add_free(v_p)
                SE.wait(v_p)
                b_y2.p(SE)
                SE.op("activation", y2row[:], accrow[:], AF.Copy,
                      accum_out=st["lnsum"][:])
                SE.op("activation", y2row[:], accrow[:], AF.Square,
                      accum_out=st["lnsumsq"][:])
                VE.wait(se_p)
                VE.op("tensor_scalar", st["lnmu"][:], st["lnsum"][:],
                      1.0 / D, None, ALU.mult)
                VE.op("tensor_scalar", st["lnvar"][:], st["lnsumsq"][:],
                      1.0 / D, None, ALU.mult)
                VE.op("tensor_tensor", st["lntmp"][:], st["lnmu"][:],
                      st["lnmu"][:], ALU.mult)
                VE.incop(v_p, "tensor_tensor", st["lnvar"][:], st["lnvar"][:],
                         st["lntmp"][:], ALU.subtract)
                SE.wait(v_p)
                SE.op("activation", st["lnrstd"][:], st["lnvar"][:], AF.Ln,
                      bias=eps_ln[:])
                SE.incop(se_p, "activation", st["lnrstd"][:], st["lnrstd"][:],
                         AF.Exp, scale=-0.5)
                VE.wait(se_p)
                VE.incop(v_p, "tensor_scalar", y2row[:], accrow[:],
                         st["lnmu"][:], st["lnrstd"][:], ALU.subtract,
                         op1=ALU.mult)
                b_acr.add_free(v_p)
                b_y2.set_ready(v_p)
                b_y2.c(SY)
                SY.incop(s_out, "dma_start", out[rt * 128:(rt + 1) * 128, :],
                         y2row[:])
                b_y2.add_free(s_out)
            SY.wait(s_out)
            self.emit_phase()
            ph.close()


        return self.nc


def kernel(**inputs):
    T = 2048
    B = inputs["L0_attn_x"].shape[0]
    prep = host_prep(inputs, T)
    b = Builder(T)
    nc = b.build()
    in_maps = []
    for i in range(B):
        m = dict(prep)
        m["x"] = np.ascontiguousarray(inputs["L0_attn_x"][i])
        in_maps.append(m)
    res = run_bass_kernel_spmd(nc, in_maps, core_ids=list(range(B)))
    return np.stack([np.asarray(res.results[i]["out"]) for i in range(B)], axis=0)
